# version 17
# baseline (speedup 1.0000x reference)
import numpy as np
import ml_dtypes
from contextlib import ExitStack

import concourse.bass as bass
import concourse.mybir as mybir
from concourse.bass_utils import run_bass_kernel_spmd

F32 = mybir.dt.float32
BF16 = mybir.dt.bfloat16
AF = mybir.ActivationFunctionType
ALU = mybir.AluOpType
AX = mybir.AxisListType
BF = ml_dtypes.bfloat16

D = 1024
II = 2048
E = 8
NGRP = 4
G = 1024
TC = 512
NK = D // 128
NIT = II // 128
NDT = D // 128
NNT = D // 128
EPS = 1e-5

ENGS = ("sync", "gpsimd", "tensor", "vector", "scalar")


def build_program(ngrp=NGRP):
    T_ = ngrp * G
    nc = bass.Bass("TRN2", target_bir_lowering=False)

    x_d = nc.dram_tensor("x", [T_, D], F32, kind="ExternalInput")
    bw_d = nc.dram_tensor("bwT", [128, NK, NNT, 128], F32, kind="ExternalInput")
    cw_d = nc.dram_tensor("cwT", [128, NNT, NDT, 128], F32, kind="ExternalInput")
    wr_d = nc.dram_tensor("wr", [128, NK, E], F32, kind="ExternalInput")
    br_d = nc.dram_tensor("br", [1, E], F32, kind="ExternalInput")
    w1_d = nc.dram_tensor("w1", [E, NIT, 128, NK, 128], F32, kind="ExternalInput")
    w3_d = nc.dram_tensor("w3", [E, NIT, 128, NK, 128], F32, kind="ExternalInput")
    w2_d = nc.dram_tensor("w2", [E, NDT, 128, NIT, 128], F32, kind="ExternalInput")
    la_d = nc.dram_tensor("la", [128, NNT], F32, kind="ExternalInput")
    bb_d = nc.dram_tensor("bb", [1, NNT, 128], F32, kind="ExternalInput")
    cb_d = nc.dram_tensor("cb", [1, NDT, 128], F32, kind="ExternalInput")
    out_d = nc.dram_tensor("out", [T_, D], F32, kind="ExternalOutput")
    w1b_d = nc.dram_tensor("w1b", [E, NIT, 128, NK, 128], BF16, kind="Internal")
    w3b_d = nc.dram_tensor("w3b", [E, NIT, 128, NK, 128], BF16, kind="Internal")
    w2b_d = nc.dram_tensor("w2b", [E, NDT, 128, NIT, 128], BF16, kind="Internal")
    eyeb_d = nc.inline_tensor(np.eye(128, dtype=np.float32).astype(BF), "eyeb")
    eyef_d = nc.inline_tensor(np.eye(128, dtype=np.float32), "eyef")

    ctx = ExitStack()
    sbuf = lambda n, s, d: ctx.enter_context(nc.sbuf_tensor(n, s, d))
    psum = lambda n, s, d: ctx.enter_context(nc.psum_tensor(n, s, d))

    x_p = sbuf("x_p", [128, 2, D], F32)
    xn_all = sbuf("xn_all", [128, 8, D], BF16)
    xnT = sbuf("xnT", [128, NK, G], BF16)
    h_s = sbuf("h_s", [128, NIT, G], BF16)
    y_acc = sbuf("y_acc", [128, NDT, G], F32)
    prodf = sbuf("prodf", [128, NDT, TC], BF16)
    wbw = sbuf("wbw", [128, NK, NNT, 128], BF16)
    wcw = sbuf("wcw", [128, NNT, NDT, 128], BF16)
    wwr = sbuf("wwr", [128, NK, E], BF16)
    wbr = sbuf("wbr", [1, E], BF16)
    w1s = sbuf("w1s", [128, 3, NK, 128], BF16)
    w3s = sbuf("w3s", [128, 2, NK, 128], BF16)
    w2s = sbuf("w2s", [128, 2, NIT, 128], BF16)
    hs_s = sbuf("hs_s", [128, 2, NNT, TC], BF16)
    silu_s = sbuf("silu_s", [128, 2, TC], F32)
    cbc = sbuf("cbc", [128, 2, TC], F32)
    crow = sbuf("crow", [1, TC], BF16)
    out_s = sbuf("out_s", [128, 3, D], F32)
    identb = sbuf("identb", [128, 128], BF16)
    identf = sbuf("identf", [128, 128], F32)
    onesb = sbuf("onesb", [1, TC], BF16)
    la_sb = sbuf("la_sb", [128, NNT], F32)
    asig = sbuf("asig", [128, NNT], F32)
    carry = sbuf("carry", [128, NNT], F32)
    stats = sbuf("stats", [128, 6], F32)
    l_sb = sbuf("l_sb", [128, 8, E], F32)
    m1_s = sbuf("m1_s", [128, 8, 1], F32)
    m2_s = sbuf("m2_s", [128, 8, 1], F32)
    mk1 = sbuf("mk1", [128, 8, E], F32)
    mk2 = sbuf("mk2", [128, 8, E], F32)
    w1r = sbuf("w1r", [128, 8, 1], F32)
    c_sb = sbuf("c_sb", [128, 8, E], F32)
    bbrow = sbuf("bbrow", [1, NNT, 128], BF16)
    cbrow = sbuf("cbrow", [1, NDT, 128], BF16)

    pb = [psum(f"pb{i}", [128, 512], F32) for i in range(8)]
    pbT = [pb[4][:, :].bitcast(BF16), pb[5][:, :].bitcast(BF16)]
    pbO = [pb[0][:, :].bitcast(BF16), pb[1][:, :].bitcast(BF16)]

    sems = {}
    for s in ("s_c", "s_cw", "s_x0", "s_x1", "s_x2", "s_p0", "s_p1",
              "s_o0", "s_o1", "s_o2",
              "s_w10", "s_w11", "s_w12", "s_w30", "s_w31",
              "s_w20", "s_w21",
              "s_b10", "s_b11", "s_b12", "s_b30", "s_b31",
              "s_b20", "s_b21",
              "s_pe", "s_dve", "s_act", "s_gp"):
        sems[s] = ctx.enter_context(nc.semaphore(s))

    sched = {e: [] for e in ENGS}
    ctr = {s: 0 for s in sems}
    last_wait = {e: {} for e in ENGS}

    def emit(eng, fn, waits=(), inc=None, n=1):
        wl = {}
        for s, v in waits:
            if v is None or v <= 0:
                continue
            if v > wl.get(s, -1):
                wl[s] = v
        wl = {s: v for s, v in wl.items() if v > last_wait[eng].get(s, 0)}
        for s, v in wl.items():
            last_wait[eng][s] = v
        sched[eng].append((fn, sorted(wl.items()), inc, n))
        if inc is not None:
            ctr[inc] += n
            return ctr[inc]
        return None

    def dma(eng, sem, out_ap, in_ap, waits=()):
        return emit(eng, lambda e, o=out_ap, i=in_ap: e.dma_start(o, i),
                    waits, sem, 16)

    dma("sync", "s_c", la_sb[:, :], la_d[:, :])
    dma("sync", "s_c", identb[:, :], eyeb_d[:, :])
    v_c = dma("sync", "s_c", identf[:, :], eyef_d[:, :])

    dma("gpsimd", "s_cw", wbw[:, :, :, :], bw_d[:, :, :, :])
    dma("gpsimd", "s_cw", wcw[:, :, :, :], cw_d[:, :, :, :])
    dma("gpsimd", "s_cw", wwr[:, :, :], wr_d[:, :, :])
    dma("gpsimd", "s_cw", wbr[:, :], br_d[:, :])
    dma("gpsimd", "s_cw", bbrow[:, :, :], bb_d[:, :, :])
    v_cw = dma("gpsimd", "s_cw", cbrow[:, :, :], cb_d[:, :, :])

    emit("gpsimd", lambda e: e.memset(onesb[:, :], 1.0), (), "s_gp")
    v_init = emit("gpsimd", lambda e: e.memset(carry[:, :], 0.0), (), "s_gp")

    v_asig = emit("scalar",
                  lambda e: e.activation(asig[:, :], la_sb[:, :], AF.Sigmoid),
                  [("s_c", v_c)], "s_act")


    ev = {}
    xp_free = {0: [], 1: []}
    outs_free = {0: [], 1: [], 2: []}
    w1_free = {0: [], 1: [], 2: []}
    w3_free = {0: [], 1: []}
    w2_free = {0: [], 1: []}

    wtabs = {"w1": (w1s, w1_d, w1b_d, w1_free),
             "w3": (w3s, w3_d, w3b_d, w3_free),
             "w2": (w2s, w2_d, w2b_d, w2_free)}

    def wload(kind, g, e_i, idx):
        sb_t, fp_d, bf_d, freel = wtabs[kind]
        b = idx % (3 if kind == "w1" else 2)
        if g == 0:
            v = dma("gpsimd", "s_%s%d" % (kind, b), sb_t[:, b, :, :],
                    fp_d[e_i, idx, :, :, :], freel[b])
            vb = dma("sync", "s_b%s%d" % (kind[1], b),
                     bf_d[e_i, idx, :, :, :], sb_t[:, b, :, :],
                     [("s_%s%d" % (kind, b), v)])
            freel[b] = [("s_b%s%d" % (kind[1], b), vb)]
        else:
            v = dma("gpsimd", "s_%s%d" % (kind, b), sb_t[:, b, :, :],
                    bf_d[e_i, idx, :, :, :], freel[b])
            freel[b] = []
        return v


    def emit_stats_load(gt):
        buf = gt % 2
        ev["xpl", gt] = dma("sync", "s_p%d" % buf, x_p[:, buf, :],
                            x_d[gt * 128:(gt + 1) * 128, :], xp_free[buf])

    def emit_stats(gt):
        buf = gt % 2
        ti = gt % 8
        vx = ev["xpl", gt]
        v1 = emit("vector",
                  lambda e, b=buf: e.reduce_sum(
                      stats[:, b:b + 1], x_p[:, b, :], AX.X, negate=True),
                  [("s_p%d" % buf, vx)], "s_dve")
        v2 = emit("vector",
                  lambda e, b=buf: e.tensor_single_scalar(
                      stats[:, b:b + 1], stats[:, b:b + 1], 1.0 / D, ALU.mult),
                  [("s_dve", v1)], "s_dve")
        wv = [("s_dve", v2)]
        if ("tp", gt - 8) in ev:
            wv.append(("s_pe", ev["tp", gt - 8]))
        v3 = emit("scalar",
                  lambda e, b=buf, t=ti: e.activation(
                      xn_all[:, t, :], x_p[:, b, :], AF.Square,
                      bias=stats[:, b:b + 1], scale=1.0,
                      accum_out=stats[:, 2 + b:3 + b]),
                  wv, "s_act")
        v4a = emit("vector",
                   lambda e, b=buf: e.tensor_scalar(
                       stats[:, 4 + b:5 + b], stats[:, 2 + b:3 + b], 1.0 / D, EPS,
                       ALU.mult, ALU.add),
                   [("s_act", v3)], "s_dve")
        v4 = emit("scalar",
                  lambda e, b=buf: e.activation(
                      stats[:, 4 + b:5 + b], stats[:, 4 + b:5 + b], AF.Sqrt),
                  [("s_dve", v4a)], "s_act")
        v5 = emit("vector",
                  lambda e, b=buf: e.reciprocal(
                      stats[:, 4 + b:5 + b], stats[:, 4 + b:5 + b]),
                  [("s_act", v4)], "s_dve")
        v6 = emit("vector",
                  lambda e, b=buf, t=ti: e.tensor_scalar(
                      xn_all[:, t, :], x_p[:, b, :], stats[:, b:b + 1],
                      stats[:, 4 + b:5 + b], ALU.add, ALU.mult),
                  [("s_dve", v5), ("s_act", v4)], "s_dve")
        ev["xn", gt] = v6
        xp_free[buf] = [("s_dve", v6)]

    def emit_P(g):
        for ti in range(8):
            gt = g * 8 + ti
            buf = ti % 2

            def tp_fn(e, t=ti, b=buf):
                ins = None
                for db in range(NK):
                    ins = e.transpose(pbT[b][:, db * 128:(db + 1) * 128],
                                      xn_all[:, t, db * 128:(db + 1) * 128],
                                      identb[:, :])
                return ins
            wv = [("s_dve", ev["xn", gt]), ("s_c", v_c)]
            if ("xnTe", gt - 2) in ev:
                wv.append(("s_dve", ev["xnTe", gt - 2]))
            if g > 0:
                wv.append(("s_dve", ev["scan", 2 * g - 1, NNT - 2 + buf]))
                if ("oadd", (g - 1) * 8 - 2 + buf) in ev:
                    wv.append(("s_dve", ev["oadd", (g - 1) * 8 - 2 + buf]))
            v7 = emit("tensor", tp_fn, wv, "s_pe")
            ev["tp", gt] = v7
            v8 = emit("vector",
                      lambda e, b=buf, t=ti: e.tensor_copy(
                          xnT[:, :, t * 128:(t + 1) * 128],
                          pbT[b][:, :].rearrange("p (k m) -> p k m", k=NK)),
                      [("s_pe", v7)], "s_dve")
            ev["xnTe", gt] = v8

        for ti in range(8):
            gt = g * 8 + ti
            pl = pb[6 + (ti % 2)]
            def lg_fn(e, t=ti, p=pl):
                for k in range(NK):
                    e.matmul(p[:, 0:E], xnT[:, k, t * 128:(t + 1) * 128],
                             wwr[:, k, :], start=(k == 0), stop=False)
                return e.matmul(p[:, 0:E], onesb[0:1, 0:128], wbr[:, :],
                                start=False, stop=True)
            wv = [("s_dve", ev["xnTe", gt]), ("s_cw", v_cw),
                  ("s_gp", v_init)]
            if ("lcp", gt - 2) in ev:
                wv.append(("s_dve", ev["lcp", gt - 2]))
            if g > 0:
                wv.append(("s_dve", ev["cbc", g - 1, E - 1, 1]))
                wv.append(("s_dve", ev["crw", g - 1, E - 1, 1]))
            v9 = emit("tensor", lg_fn, wv, "s_pe")
            v10 = emit("vector",
                       lambda e, t=ti, p=pl: e.tensor_copy(
                           l_sb[:, t, :], p[:, 0:E]),
                       [("s_pe", v9)], "s_dve")
            ev["lcp", gt] = v10

        last_l = ev["lcp", g * 8 + 7]
        r = emit("vector", lambda e: e.reduce_max(
            m1_s[:, :, :], l_sb[:, :, :], AX.X), [("s_dve", last_l)], "s_dve")
        r = emit("vector", lambda e: e.tensor_tensor(
            mk1[:, :, :], l_sb[:, :, :], m1_s[:, :, :].broadcast_to([128, 8, E]),
            ALU.is_ge), [("s_dve", r)], "s_dve")
        r = emit("vector", lambda e: e.scalar_tensor_tensor(
            l_sb[:, :, :], mk1[:, :, :], -1e30, l_sb[:, :, :],
            ALU.mult, ALU.add), [("s_dve", r)], "s_dve")
        r = emit("vector", lambda e: e.reduce_max(
            m2_s[:, :, :], l_sb[:, :, :], AX.X), [("s_dve", r)], "s_dve")
        r = emit("vector", lambda e: e.tensor_tensor(
            mk2[:, :, :], l_sb[:, :, :], m2_s[:, :, :].broadcast_to([128, 8, E]),
            ALU.is_ge), [("s_dve", r)], "s_dve")
        r = emit("vector", lambda e: e.tensor_sub(
            m2_s[:, :, :], m1_s[:, :, :], m2_s[:, :, :]), [("s_dve", r)], "s_dve")
        r7 = emit("scalar", lambda e: e.activation(
            w1r[:, :, :], m2_s[:, :, :], AF.Sigmoid), [("s_dve", r)], "s_act")
        r = emit("vector", lambda e: e.tensor_scalar(
            m1_s[:, :, :], w1r[:, :, :], -1.0, 1.0, ALU.mult, ALU.add),
            [("s_act", r7)], "s_dve")
        r = emit("vector", lambda e: e.tensor_tensor(
            mk2[:, :, :], mk2[:, :, :], m1_s[:, :, :].broadcast_to([128, 8, E]),
            ALU.mult), [("s_dve", r)], "s_dve")
        r = emit("vector", lambda e: e.tensor_tensor(
            c_sb[:, :, :], mk1[:, :, :], w1r[:, :, :].broadcast_to([128, 8, E]),
            ALU.mult), [("s_dve", r)], "s_dve")
        r = emit("vector", lambda e: e.tensor_add(
            c_sb[:, :, :], c_sb[:, :, :], mk2[:, :, :]), [("s_dve", r)], "s_dve")
        ev["c", g] = r

    for g in range(ngrp):
        if g == 0:
            emit_stats_load(0)
            emit_stats_load(1)
            for ti in range(8):
                emit_stats(ti)
                if ti + 2 < 8:
                    emit_stats_load(ti + 2)
            emit_P(0)
        prev_yacc = ev.get(("yacc", E - 1, NDT - 1, 1))
        for k in list(ev):
            if k[0] in ("gate", "up", "silu", "h", "y", "ym", "yacc",
                        "w2ld"):
                del ev[k]
        for e_i in range(E):
            def emit_ctp(tc, ei=None):
                ei = e_i if ei is None else ei
                def ctp_fn(e, ei_=ei, t=tc):
                    ins = None
                    for tb in range(4):
                        ins = e.transpose(
                            pb[7][0:1, tb * 128:(tb + 1) * 128],
                            c_sb[:, t * 4 + tb, ei_:ei_ + 1], identf[:, :])
                    return ins
                wv = [("s_dve", ev["c", g]), ("s_c", v_c)]
                if tc == 1:
                    wv.append(("s_dve", ev["crw", g, ei, 0]))
                elif ei > 0:
                    wv.append(("s_dve", ev["crw", g, ei - 1, 1]))
                vct = emit("tensor", ctp_fn, wv, "s_pe")
                vcr = emit("vector",
                           lambda e: e.tensor_copy(
                               crow[:, :], pb[7][0:1, :]),
                           [("s_pe", vct)], "s_dve")
                ev["crw", g, ei, tc] = vcr

            def emit_cbc(tc, ei=None):
                ei = e_i if ei is None else ei
                wv = [("s_dve", ev["crw", g, ei, tc])]
                if tc == 1:
                    wv.append(("s_dve", ev["cbc", g, ei, 0]))
                elif ei > 0:
                    wv.append(("s_dve", ev["cbc", g, ei - 1, 1]))
                vcm = emit("tensor",
                           lambda e: e.matmul(
                               pb[6][:, :], onesb[0:1, 0:128], crow[0:1, :],
                               start=True, stop=True),
                           wv, "s_pe")
                wv = [("s_pe", vcm)]
                if ei > 0:
                    wv.append(("s_dve", ev["yacc", ei - 1, NDT - 1, 1]))
                vcb = emit("vector",
                           lambda e, t=tc: e.tensor_copy(
                               cbc[:, t, :], pb[6][:, :]),
                           wv, "s_dve")
                ev["cbc", g, ei, tc] = vcb

            for fn_c, tc_c in [(emit_ctp, 0), (emit_cbc, 0), (emit_ctp, 1),
                               (emit_cbc, 1)]:
                fn_c(tc_c)

            for it in range(NIT):
                b = it % 2
                b1 = it % 3
                if ("w1ld", e_i, it) in ev:
                    vw1 = ev["w1ld", e_i, it]
                    vw3 = ev["w3ld", e_i, it]
                else:
                    vw1 = wload("w1", g, e_i, it)
                    vw3 = wload("w3", g, e_i, it)

                def gate_fn(e, bb_=b1):
                    ins = None
                    for k in range(NK):
                        for tc in range(2):
                            ins = e.matmul(
                                pb[tc][:, :], w1s[:, bb_, k, :],
                                xnT[:, k, tc * TC:(tc + 1) * TC],
                                start=(k == 0), stop=(k == NK - 1))
                    return ins
                wv = [("s_w1%d" % b1, vw1), ("s_dve", ev["xnTe", g * 8 + 7])]
                if ("silu", e_i, it - 1) in ev:
                    wv.append(("s_act", ev["silu", e_i, it - 1]))
                elif e_i > 0:
                    wv.append(("s_act", ev["silu", e_i - 1, NIT - 1]))
                elif g > 0:
                    wv.append(("s_dve", ev["oadd", g * 8 - 5]))
                vg = emit("tensor", gate_fn, wv, "s_pe")
                ev["gate", e_i, it] = vg
                w1_free[b1] = w1_free[b1] + [("s_pe", vg)]

                def up_fn(e, bb_=b):
                    ins = None
                    for k in range(NK):
                        for tc in range(2):
                            ins = e.matmul(
                                pb[2 + tc][:, :], w3s[:, bb_, k, :],
                                xnT[:, k, tc * TC:(tc + 1) * TC],
                                start=(k == 0), stop=(k == NK - 1))
                    return ins
                wv = [("s_w3%d" % b, vw3)]
                if ("h", e_i, it - 1) in ev:
                    wv.append(("s_dve", ev["h", e_i, it - 1]))
                elif e_i > 0:
                    wv.append(("s_dve", ev["h", e_i - 1, NIT - 1]))
                elif g > 0:
                    wv.append(("s_dve", ev["prod", 2 * g - 1, NDT - 1]))
                vu = emit("tensor", up_fn, wv, "s_pe")
                ev["up", e_i, it] = vu
                w3_free[b] = w3_free[b] + [("s_pe", vu)]

                def silu_fn(e):
                    e.activation(silu_s[:, 0, :], pb[0][:, :], AF.Silu)
                    return e.activation(silu_s[:, 1, :], pb[1][:, :], AF.Silu)
                wv = [("s_pe", vg)]
                if ("h", e_i, it - 1) in ev:
                    wv.append(("s_dve", ev["h", e_i, it - 1]))
                elif e_i > 0:
                    wv.append(("s_dve", ev["h", e_i - 1, NIT - 1]))
                    wv.append(("s_dve", ev["yacc", e_i - 1, NDT - 1, 1]))
                elif prev_yacc is not None:
                    wv.append(("s_dve", prev_yacc))
                vs = emit("scalar", silu_fn, wv, "s_act")
                ev["silu", e_i, it] = vs

                def h_fn(e, i=it):
                    e.tensor_tensor(h_s[:, i, 0:TC], silu_s[:, 0, :],
                                    pb[2][:, :], ALU.mult)
                    return e.tensor_tensor(h_s[:, i, TC:G], silu_s[:, 1, :],
                                           pb[3][:, :], ALU.mult)
                vh = emit("vector", h_fn,
                          [("s_act", vs), ("s_pe", vu)], "s_dve")
                ev["h", e_i, it] = vh
                if it == 0 and g + 1 < ngrp:
                    emit_stats_load((g + 1) * 8 + e_i)
                if it == 6 and g + 1 < ngrp:
                    emit_stats((g + 1) * 8 + e_i)
                if 12 <= it < 14:
                    dt_p = it - 12
                    ev["w2ld", e_i, dt_p] = wload("w2", g, e_i, dt_p)

            for dt in range(NDT):
                b = dt % 2
                if dt in (2, 4) and e_i + 1 < E:
                    itp = dt // 2 - 1
                    ev["w1ld", e_i + 1, itp] = wload("w1", g, e_i + 1, itp)
                    ev["w3ld", e_i + 1, itp] = wload("w3", g, e_i + 1, itp)
                if ("w2ld", e_i, dt) in ev:
                    vw2 = ev["w2ld", e_i, dt]
                else:
                    vw2 = wload("w2", g, e_i, dt)

                for tc in range(2):
                    pbank = pb[4 + (2 * dt + tc) % 2]
                    wv = [("s_w2%d" % b, vw2),
                          ("s_dve", ev["h", e_i, NIT - 1])]
                    if ("ym", e_i, dt - 1, tc) in ev:
                        wv.append(("s_dve", ev["ym", e_i, dt - 1, tc]))
                    elif e_i > 0:
                        wv.append(("s_dve", ev["ym", e_i - 1, NDT - 1, tc]))
                    elif g > 0:
                        wv.append(("s_dve", ev["oadd", g * 8 - 2 + tc]))
                        wv.append(("s_dve", ev["xnTe", g * 8 + 7]))
                    if dt == 0:
                        def ya_fn(e, bb_=b, t=tc, p=pbank):
                            ins = None
                            for kk in range(NIT - 2):
                                ins = e.matmul(
                                    p[:, :], w2s[:, bb_, kk, :],
                                    h_s[:, kk, t * TC:(t + 1) * TC],
                                    start=(kk == 0), stop=False)
                            return ins
                        wv = [w if w != ("s_dve", ev["h", e_i, NIT - 1])
                              else ("s_dve", ev["h", e_i, NIT - 3])
                              for w in wv]
                        emit("tensor", ya_fn, wv, "s_pe")
                        def yb_fn(e, bb_=b, t=tc, p=pbank):
                            ins = None
                            for kk in range(NIT - 2, NIT):
                                ins = e.matmul(
                                    p[:, :], w2s[:, bb_, kk, :],
                                    h_s[:, kk, t * TC:(t + 1) * TC],
                                    start=False, stop=(kk == NIT - 1))
                            return ins
                        vy = emit("tensor", yb_fn,
                                  [("s_dve", ev["h", e_i, NIT - 1])], "s_pe")
                    else:
                        def y_fn(e, bb_=b, t=tc, p=pbank):
                            ins = None
                            for kk in range(NIT):
                                ins = e.matmul(
                                    p[:, :], w2s[:, bb_, kk, :],
                                    h_s[:, kk, t * TC:(t + 1) * TC],
                                    start=(kk == 0), stop=(kk == NIT - 1))
                            return ins
                        vy = emit("tensor", y_fn, wv, "s_pe")
                    ev["y", e_i, dt, tc] = vy

                    wv = [("s_pe", vy), ("s_dve", ev["cbc", g, e_i, 1])]
                    if ("yacc", e_i, dt, tc - 1) in ev:
                        wv.append(("s_dve", ev["yacc", e_i, dt, tc - 1]))
                    elif ("yacc", e_i, dt - 1, 1) in ev:
                        wv.append(("s_dve", ev["yacc", e_i, dt - 1, 1]))
                    if e_i == 0:
                        vm = emit("vector",
                                  lambda e, d=dt, t=tc, p=pbank:
                                  e.tensor_tensor(
                                      y_acc[:, d, t * TC:(t + 1) * TC],
                                      p[:, :], cbc[:, t, :], ALU.mult),
                                  wv, "s_dve")
                        vye = vm
                    else:
                        vm = emit("vector",
                                  lambda e, t=tc, p=pbank: e.tensor_tensor(
                                      silu_s[:, t, :], p[:, :], cbc[:, t, :],
                                      ALU.mult), wv, "s_dve")
                        vye = emit("vector",
                                   lambda e, d=dt, t=tc: e.tensor_add(
                                       y_acc[:, d, t * TC:(t + 1) * TC],
                                       y_acc[:, d, t * TC:(t + 1) * TC],
                                       silu_s[:, t, :]),
                                   [("s_dve", vm)], "s_dve")
                    ev["ym", e_i, dt, tc] = vm
                    ev["yacc", e_i, dt, tc] = vye
                w2_free[b] = w2_free[b] + [("s_pe", ev["y", e_i, dt, 1])]

        def emit_bx_scan(tc):
            gc = 2 * g + tc
            sl = gc % 2
            for nt in range(NNT):
                pbB = pb[4 + nt % 2]
                def bx_fn(e, n=nt, t=tc, p=pbB):
                    for k in range(NK):
                        e.matmul(p[:, :], wbw[:, k, n, :],
                                 xnT[:, k, t * TC:(t + 1) * TC],
                                 start=(k == 0), stop=False)
                    return e.matmul(p[:, :], bbrow[0:1, n, :], onesb[0:1, :],
                                    start=False, stop=True)
                wv = [("s_cw", v_cw), ("s_gp", v_init)]
                if nt < 2 and tc == 0:
                    wv.append(("s_dve", ev["ym", E - 1, NDT - 1, nt]))
                if ("scan", gc, nt - 2) in ev:
                    wv.append(("s_dve", ev["scan", gc, nt - 2]))
                elif tc == 1:
                    wv.append(("s_dve", ev["scan", gc - 1, NNT - 2 + nt]))
                vb = emit("tensor", bx_fn, wv, "s_pe")
                ev["bx", gc, nt] = vb

                wv = [("s_pe", vb), ("s_gp", v_init)]
                if gc > 1:
                    wv.append(("s_pe", ev["cmm", gc - 2, NDT - 1]))
                if ("scan", gc - 1, nt) in ev:
                    wv.append(("s_dve", ev["scan", gc - 1, nt]))
                vsc = emit("vector",
                           lambda e, n=nt, p=pbB, s=sl: e.tensor_tensor_scan(
                               hs_s[:, s, n, :],
                               asig[:, n:n + 1].broadcast_to([128, TC]),
                               p[:, :], carry[:, n:n + 1],
                               ALU.mult, ALU.add),
                           wv, "s_dve")
                vcy = emit("vector",
                           lambda e, n=nt, s=sl: e.tensor_copy(
                               carry[:, n:n + 1], hs_s[:, s, n, TC - 1:TC]),
                           [("s_dve", vsc)], "s_dve")
                ev["scan", gc, nt] = vcy

        def emit_cs(tc):
            gc = 2 * g + tc
            sl = gc % 2
            for tb in range(3):
                ot = gc * 4 + tb
                obuf = ot % 3
                ev["xload", ot] = dma(
                    "sync", "s_x%d" % obuf, out_s[:, obuf, :],
                    x_d[ot * 128:(ot + 1) * 128, :], outs_free[obuf])
            for dt in range(NDT):
                pbS = pb[2 + dt % 2]
                def c_fn(e, d=dt, p=pbS, s=sl):
                    for kn in range(NNT):
                        e.matmul(p[:, :], wcw[:, kn, d, :], hs_s[:, s, kn, :],
                                 start=(kn == 0), stop=False)
                    return e.matmul(p[:, :], cbrow[0:1, d, :], onesb[0:1, :],
                                    start=False, stop=True)
                wv = [("s_cw", v_cw), ("s_dve", ev["scan", gc, NNT - 1])]
                if ("prod", gc, dt - 2) in ev:
                    wv.append(("s_dve", ev["prod", gc, dt - 2]))
                elif tc == 1:
                    wv.append(("s_dve", ev["prod", gc - 1, NDT - 2 + dt]))
                elif dt < 2:
                    wv.append(("s_dve", ev["h", E - 1, NIT - 1]))
                vc = emit("tensor", c_fn, wv, "s_pe")
                ev["cmm", gc, dt] = vc

                wv = [("s_dve", ev["yacc", E - 1, dt, tc])]
                if gc > 0:
                    wv.append(("s_pe", ev["ptp", gc - 1, 3]))
                vgs = emit("scalar",
                           lambda e, d=dt, t=tc: e.activation(
                               prodf[:, d, :], y_acc[:, d, t * TC:(t + 1) * TC],
                               AF.Silu),
                           wv, "s_act")
                vp = emit("vector",
                          lambda e, d=dt, p=pbS: e.tensor_tensor(
                              prodf[:, d, :], prodf[:, d, :],
                              p[:, :], ALU.mult),
                          [("s_act", vgs), ("s_pe", vc)], "s_dve")
                ev["prod", gc, dt] = vp

        def emit_epi(tc):
            gc = 2 * g + tc
            banks = pbO if tc == 0 else pbT
            for tb in range(4):
                ot = gc * 4 + tb
                obuf = ot % 3
                if ("xload", ot) in ev:
                    vxr = ev["xload", ot]
                else:
                    vxr = dma("sync", "s_x%d" % obuf, out_s[:, obuf, :],
                              x_d[ot * 128:(ot + 1) * 128, :],
                              outs_free[obuf])
                def ptp_fn(e, t=tb, b=tb % 2, bk=banks):
                    ins = None
                    for dt in range(NDT):
                        ins = e.transpose(
                            bk[b][:, dt * 128:(dt + 1) * 128],
                            prodf[:, dt, t * 128:(t + 1) * 128], identb[:, :])
                    return ins
                wv = [("s_dve", ev["prod", gc, NDT - 1])]
                if ("oadd", ot - 2) in ev:
                    wv.append(("s_dve", ev["oadd", ot - 2]))
                if tb == 0 and tc == 0:
                    wv.append(("s_act", ev["silu", E - 1, NIT - 1]))
                if tc == 1 and tb < 2:
                    if g + 1 < ngrp:
                        wv.append(("s_dve", ev["xnTe", (g + 1) * 8 + 7]))
                    else:
                        wv.append(("s_dve", ev["scan", gc, NNT - 2 + tb]))
                vtp = emit("tensor", ptp_fn, wv, "s_pe")
                ev["ptp", gc, tb] = vtp

                voa = emit("vector",
                           lambda e, b=obuf, k=tb % 2, bk=banks: e.tensor_add(
                               out_s[:, b, :], out_s[:, b, :], bk[k][:, :]),
                           [("s_pe", vtp), ("s_x%d" % obuf, vxr)], "s_dve")
                ev["oadd", ot] = voa
                vst = dma("sync", "s_o%d" % obuf,
                          out_d[ot * 128:(ot + 1) * 128, :],
                          out_s[:, obuf, :], [("s_dve", voa)])
                outs_free[obuf] = [("s_o%d" % obuf, vst)]

        emit_bx_scan(0)
        emit_bx_scan(1)
        emit_cs(0)
        emit_epi(0)
        emit_cs(1)
        if g + 1 < ngrp:
            emit_P(g + 1)
        emit_epi(1)

    emit("sync", lambda e: e.nop(),
         [("s_o0", ctr["s_o0"]), ("s_o1", ctr["s_o1"]),
          ("s_o2", ctr["s_o2"])])

    block = ctx.enter_context(nc.Block())

    def replay(name):
        def f(eng):
            for fn, waits, inc, n in sched[name]:
                for s, v in waits:
                    eng.wait_ge(sems[s], v)
                ins = fn(eng)
                if inc is not None:
                    ins.then_inc(sems[inc], n)
        return f

    block.sync(replay("sync"))
    block.gpsimd(replay("gpsimd"))
    block.tensor(replay("tensor"))
    block.vector(replay("vector"))
    block.scalar(replay("scalar"))
    ctx.close()
    return nc


def prep_inputs(L0_attn_x, L0_attn_log_A, L0_attn_B_w, L0_attn_B_b,
                L0_attn_C_w, L0_attn_C_b, L0_ffn_Wr, L0_ffn_br,
                L0_ffn_W1, L0_ffn_W3, L0_ffn_W2):
    C = np.ascontiguousarray
    f = np.float32
    shared = {
        "bwT": C(np.asarray(L0_attn_B_w, f).T.reshape(NK, 128, NNT, 128)
                 .transpose(1, 0, 2, 3)),
        "cwT": C(np.asarray(L0_attn_C_w, f).T.reshape(NNT, 128, NDT, 128)
                 .transpose(1, 0, 2, 3)),
        "wr": C(np.asarray(L0_ffn_Wr, f).reshape(NK, 128, E).transpose(1, 0, 2)),
        "br": C(np.asarray(L0_ffn_br, f).reshape(1, E)),
        "w1": C(np.asarray(L0_ffn_W1, f).reshape(E, NK, 128, NIT, 128)
                .transpose(0, 3, 2, 1, 4)),
        "w3": C(np.asarray(L0_ffn_W3, f).reshape(E, NK, 128, NIT, 128)
                .transpose(0, 3, 2, 1, 4)),
        "w2": C(np.asarray(L0_ffn_W2, f).reshape(E, NIT, 128, NDT, 128)
                .transpose(0, 3, 2, 1, 4)),
        "la": C(np.asarray(L0_attn_log_A, f).reshape(NNT, 128).T),
        "bb": C(np.asarray(L0_attn_B_b, f).reshape(1, NNT, 128)),
        "cb": C(np.asarray(L0_attn_C_b, f).reshape(1, NDT, 128)),
    }
    xs = np.asarray(L0_attn_x, f)
    in_maps = [dict(shared, x=C(xs[b])) for b in range(xs.shape[0])]
    return in_maps


def kernel(**inputs):
    in_maps = prep_inputs(**inputs)
    nc = build_program()
    res = run_bass_kernel_spmd(nc, in_maps, core_ids=list(range(8)))
    out = np.stack([np.asarray(r["out"], dtype=np.float32)
                    for r in res.results], axis=0)
    return out


# version 18
# speedup vs baseline: 1.1352x; 1.1352x over previous
import numpy as np
import ml_dtypes
from contextlib import ExitStack

import concourse.bass as bass
import concourse.mybir as mybir
from concourse.bass_utils import run_bass_kernel_spmd

F32 = mybir.dt.float32
BF16 = mybir.dt.bfloat16
AF = mybir.ActivationFunctionType
ALU = mybir.AluOpType
AX = mybir.AxisListType
BF = ml_dtypes.bfloat16

D = 1024
II = 2048
E = 8
NGRP = 4
G = 1024
TC = 512
NK = D // 128
NIT = II // 128
NDT = D // 128
NNT = D // 128
EPS = 1e-5

ENGS = ("sync", "gpsimd", "tensor", "vector", "scalar")


def build_program(ngrp=NGRP):
    T_ = ngrp * G
    nc = bass.Bass("TRN2", target_bir_lowering=False)

    x_d = nc.dram_tensor("x", [T_, D], F32, kind="ExternalInput")
    bw_d = nc.dram_tensor("bwT", [128, NK, NNT, 128], F32, kind="ExternalInput")
    cw_d = nc.dram_tensor("cwT", [128, NNT, NDT, 128], F32, kind="ExternalInput")
    wr_d = nc.dram_tensor("wr", [128, NK, E], F32, kind="ExternalInput")
    br_d = nc.dram_tensor("br", [1, E], F32, kind="ExternalInput")
    w1_d = nc.dram_tensor("w1", [E, NIT, 128, NK, 128], F32, kind="ExternalInput")
    w3_d = nc.dram_tensor("w3", [E, NIT, 128, NK, 128], F32, kind="ExternalInput")
    w2_d = nc.dram_tensor("w2", [E, NDT, 128, NIT, 128], F32, kind="ExternalInput")
    la_d = nc.dram_tensor("la", [128, NNT], F32, kind="ExternalInput")
    bb_d = nc.dram_tensor("bb", [1, NNT, 128], F32, kind="ExternalInput")
    cb_d = nc.dram_tensor("cb", [1, NDT, 128], F32, kind="ExternalInput")
    out_d = nc.dram_tensor("out", [T_, D], F32, kind="ExternalOutput")
    eyeb_d = nc.inline_tensor(np.eye(128, dtype=np.float32).astype(BF), "eyeb")
    eyef_d = nc.inline_tensor(np.eye(128, dtype=np.float32), "eyef")

    ctx = ExitStack()
    sbuf = lambda n, s, d: ctx.enter_context(nc.sbuf_tensor(n, s, d))
    psum = lambda n, s, d: ctx.enter_context(nc.psum_tensor(n, s, d))

    x_p = sbuf("x_p", [128, 2, D], F32)
    xn_all = sbuf("xn_all", [128, 8, D], BF16)
    xnT = sbuf("xnT", [128, NK, G], BF16)
    h_s = sbuf("h_s", [128, NIT, G], BF16)
    y_acc = sbuf("y_acc", [128, NDT, G], F32)
    prodf = sbuf("prodf", [128, NDT, TC], BF16)
    wbw = sbuf("wbw", [128, NK, NNT, 128], BF16)
    wcw = sbuf("wcw", [128, NNT, NDT, 128], BF16)
    wwr = sbuf("wwr", [128, NK, E], BF16)
    wbr = sbuf("wbr", [1, E], BF16)
    w1s = sbuf("w1s", [128, 2, NK, 128], BF16)
    w3s = sbuf("w3s", [128, 2, NK, 128], BF16)
    w2s = sbuf("w2s", [128, 2, NIT, 128], BF16)
    hs_s = sbuf("hs_s", [128, 2, NNT, TC], BF16)
    silu_s = sbuf("silu_s", [128, 2, TC], F32)
    cbc = sbuf("cbc", [128, 2, TC], F32)
    crow = sbuf("crow", [1, TC], F32)
    out_s = sbuf("out_s", [128, 3, D], F32)
    identb = sbuf("identb", [128, 128], BF16)
    identf = sbuf("identf", [128, 128], F32)
    onesb = sbuf("onesb", [1, TC], BF16)
    onesf = sbuf("onesf", [1, 128], F32)
    la_sb = sbuf("la_sb", [128, NNT], F32)
    asig = sbuf("asig", [128, NNT], F32)
    carry = sbuf("carry", [128, NNT], F32)
    stats = sbuf("stats", [128, 6], F32)
    l_sb = sbuf("l_sb", [128, 8, E], F32)
    m1_s = sbuf("m1_s", [128, 8, 1], F32)
    m2_s = sbuf("m2_s", [128, 8, 1], F32)
    mk1 = sbuf("mk1", [128, 8, E], F32)
    mk2 = sbuf("mk2", [128, 8, E], F32)
    w1r = sbuf("w1r", [128, 8, 1], F32)
    c_sb = sbuf("c_sb", [128, 8, E], F32)
    bbrow = sbuf("bbrow", [1, NNT, 128], BF16)
    cbrow = sbuf("cbrow", [1, NDT, 128], BF16)

    pb = [psum(f"pb{i}", [128, 512], F32) for i in range(8)]
    pbT = [pb[4][:, :].bitcast(BF16), pb[5][:, :].bitcast(BF16)]
    pbO = [pb[0][:, :].bitcast(BF16), pb[1][:, :].bitcast(BF16)]

    sems = {}
    for s in ("s_c", "s_cw", "s_x0", "s_x1", "s_x2", "s_p0", "s_p1",
              "s_o0", "s_o1", "s_o2",
              "s_w10", "s_w11", "s_w30", "s_w31", "s_w20", "s_w21",
              "s_pe", "s_dve", "s_act", "s_gp"):
        sems[s] = ctx.enter_context(nc.semaphore(s))

    sched = {e: [] for e in ENGS}
    ctr = {s: 0 for s in sems}
    last_wait = {e: {} for e in ENGS}

    def emit(eng, fn, waits=(), inc=None, n=1):
        wl = {}
        for s, v in waits:
            if v is None or v <= 0:
                continue
            if v > wl.get(s, -1):
                wl[s] = v
        wl = {s: v for s, v in wl.items() if v > last_wait[eng].get(s, 0)}
        for s, v in wl.items():
            last_wait[eng][s] = v
        sched[eng].append((fn, sorted(wl.items()), inc, n))
        if inc is not None:
            ctr[inc] += n
            return ctr[inc]
        return None

    def dma(eng, sem, out_ap, in_ap, waits=()):
        return emit(eng, lambda e, o=out_ap, i=in_ap: e.dma_start(o, i),
                    waits, sem, 16)

    dma("sync", "s_c", la_sb[:, :], la_d[:, :])
    dma("sync", "s_c", identb[:, :], eyeb_d[:, :])
    v_c = dma("sync", "s_c", identf[:, :], eyef_d[:, :])

    dma("gpsimd", "s_cw", wbw[:, :, :, :], bw_d[:, :, :, :])
    dma("gpsimd", "s_cw", wcw[:, :, :, :], cw_d[:, :, :, :])
    dma("gpsimd", "s_cw", wwr[:, :, :], wr_d[:, :, :])
    dma("gpsimd", "s_cw", wbr[:, :], br_d[:, :])
    dma("gpsimd", "s_cw", bbrow[:, :, :], bb_d[:, :, :])
    v_cw = dma("gpsimd", "s_cw", cbrow[:, :, :], cb_d[:, :, :])

    emit("gpsimd", lambda e: e.memset(onesb[:, :], 1.0), (), "s_gp")
    emit("gpsimd", lambda e: e.memset(onesf[:, :], 1.0), (), "s_gp")
    v_init = emit("gpsimd", lambda e: e.memset(carry[:, :], 0.0), (), "s_gp")

    v_asig = emit("scalar",
                  lambda e: e.activation(asig[:, :], la_sb[:, :], AF.Sigmoid),
                  [("s_c", v_c)], "s_act")

    ev = {}
    xp_free = {0: [], 1: []}
    outs_free = {0: [], 1: [], 2: []}
    w1_free = {0: [], 1: []}
    w3_free = {0: [], 1: []}
    w2_free = {0: [], 1: []}

    def emit_stats(gt):
        buf = gt % 2
        ti = gt % 8
        vx = dma("sync", "s_p%d" % buf, x_p[:, buf, :],
                 x_d[gt * 128:(gt + 1) * 128, :], xp_free[buf])
        v1 = emit("vector",
                  lambda e, b=buf: e.reduce_sum(
                      stats[:, b:b + 1], x_p[:, b, :], AX.X, negate=True),
                  [("s_p%d" % buf, vx)], "s_dve")
        v2 = emit("vector",
                  lambda e, b=buf: e.tensor_single_scalar(
                      stats[:, b:b + 1], stats[:, b:b + 1], 1.0 / D, ALU.mult),
                  [("s_dve", v1)], "s_dve")
        wv = [("s_dve", v2)]
        if ("tp", gt - 8) in ev:
            wv.append(("s_pe", ev["tp", gt - 8]))
        v3 = emit("scalar",
                  lambda e, b=buf, t=ti: e.activation(
                      xn_all[:, t, :], x_p[:, b, :], AF.Square,
                      bias=stats[:, b:b + 1], scale=1.0,
                      accum_out=stats[:, 2 + b:3 + b]),
                  wv, "s_act")
        v4a = emit("vector",
                   lambda e, b=buf: e.tensor_scalar(
                       stats[:, 4 + b:5 + b], stats[:, 2 + b:3 + b], 1.0 / D, EPS,
                       ALU.mult, ALU.add),
                   [("s_act", v3)], "s_dve")
        v4 = emit("scalar",
                  lambda e, b=buf: e.activation(
                      stats[:, 4 + b:5 + b], stats[:, 4 + b:5 + b], AF.Sqrt),
                  [("s_dve", v4a)], "s_act")
        v5 = emit("vector",
                  lambda e, b=buf: e.reciprocal(
                      stats[:, 4 + b:5 + b], stats[:, 4 + b:5 + b]),
                  [("s_act", v4)], "s_dve")
        v6 = emit("vector",
                  lambda e, b=buf, t=ti: e.tensor_scalar(
                      xn_all[:, t, :], x_p[:, b, :], stats[:, b:b + 1],
                      stats[:, 4 + b:5 + b], ALU.add, ALU.mult),
                  [("s_dve", v5), ("s_act", v4)], "s_dve")
        ev["xn", gt] = v6
        xp_free[buf] = [("s_dve", v6)]

    for g in range(ngrp):
        if g == 0:
            for ti in range(8):
                emit_stats(ti)
        for ti in range(8):
            gt = g * 8 + ti
            buf = ti % 2

            def tp_fn(e, t=ti, b=buf):
                ins = None
                for db in range(NK):
                    ins = e.transpose(pbT[b][:, db * 128:(db + 1) * 128],
                                      xn_all[:, t, db * 128:(db + 1) * 128],
                                      identb[:, :])
                return ins
            wv = [("s_dve", ev["xn", gt]), ("s_c", v_c)]
            if ("xnTe", gt - 2) in ev:
                wv.append(("s_dve", ev["xnTe", gt - 2]))
            if g > 0:
                wv.append(("s_dve", ev["scan", 2 * g - 1, NNT - 2 + buf]))
            v7 = emit("tensor", tp_fn, wv, "s_pe")
            ev["tp", gt] = v7
            v8 = emit("vector",
                      lambda e, b=buf, t=ti: e.tensor_copy(
                          xnT[:, :, t * 128:(t + 1) * 128],
                          pbT[b][:, :].rearrange("p (k m) -> p k m", k=NK)),
                      [("s_pe", v7)], "s_dve")
            ev["xnTe", gt] = v8

        for ti in range(8):
            gt = g * 8 + ti
            pl = pb[6 + (ti % 2)]
            def lg_fn(e, t=ti, p=pl):
                for k in range(NK):
                    e.matmul(p[:, 0:E], xnT[:, k, t * 128:(t + 1) * 128],
                             wwr[:, k, :], start=(k == 0), stop=False)
                return e.matmul(p[:, 0:E], onesb[0:1, 0:128], wbr[:, :],
                                start=False, stop=True)
            wv = [("s_dve", ev["xnTe", gt]), ("s_cw", v_cw),
                  ("s_gp", v_init)]
            if ("lcp", gt - 2) in ev:
                wv.append(("s_dve", ev["lcp", gt - 2]))
            if g > 0:
                wv.append(("s_dve", ev["cbc", g - 1, E - 1, 1]))
                wv.append(("s_dve", ev["crw", g - 1, E - 1, 1]))
            v9 = emit("tensor", lg_fn, wv, "s_pe")
            v10 = emit("vector",
                       lambda e, t=ti, p=pl: e.tensor_copy(
                           l_sb[:, t, :], p[:, 0:E]),
                       [("s_pe", v9)], "s_dve")
            ev["lcp", gt] = v10

        last_l = ev["lcp", g * 8 + 7]
        r = emit("vector", lambda e: e.reduce_max(
            m1_s[:, :, :], l_sb[:, :, :], AX.X), [("s_dve", last_l)], "s_dve")
        r = emit("vector", lambda e: e.tensor_tensor(
            mk1[:, :, :], l_sb[:, :, :], m1_s[:, :, :].broadcast_to([128, 8, E]),
            ALU.is_ge), [("s_dve", r)], "s_dve")
        r = emit("vector", lambda e: e.scalar_tensor_tensor(
            l_sb[:, :, :], mk1[:, :, :], -1e30, l_sb[:, :, :],
            ALU.mult, ALU.add), [("s_dve", r)], "s_dve")
        r = emit("vector", lambda e: e.reduce_max(
            m2_s[:, :, :], l_sb[:, :, :], AX.X), [("s_dve", r)], "s_dve")
        r = emit("vector", lambda e: e.tensor_tensor(
            mk2[:, :, :], l_sb[:, :, :], m2_s[:, :, :].broadcast_to([128, 8, E]),
            ALU.is_ge), [("s_dve", r)], "s_dve")
        r = emit("vector", lambda e: e.tensor_sub(
            m2_s[:, :, :], m1_s[:, :, :], m2_s[:, :, :]), [("s_dve", r)], "s_dve")
        r7 = emit("scalar", lambda e: e.activation(
            w1r[:, :, :], m2_s[:, :, :], AF.Sigmoid), [("s_dve", r)], "s_act")
        r = emit("vector", lambda e: e.tensor_scalar(
            m1_s[:, :, :], w1r[:, :, :], -1.0, 1.0, ALU.mult, ALU.add),
            [("s_act", r7)], "s_dve")
        r = emit("vector", lambda e: e.tensor_tensor(
            mk2[:, :, :], mk2[:, :, :], m1_s[:, :, :].broadcast_to([128, 8, E]),
            ALU.mult), [("s_dve", r)], "s_dve")
        r = emit("vector", lambda e: e.tensor_tensor(
            c_sb[:, :, :], mk1[:, :, :], w1r[:, :, :].broadcast_to([128, 8, E]),
            ALU.mult), [("s_dve", r)], "s_dve")
        r = emit("vector", lambda e: e.tensor_add(
            c_sb[:, :, :], c_sb[:, :, :], mk2[:, :, :]), [("s_dve", r)], "s_dve")
        ev["c", g] = r

        prev_yacc = ev.get(("yacc", E - 1, NDT - 1, 1))
        for k in list(ev):
            if k[0] in ("gate", "up", "silu", "h", "y", "ym", "yacc",
                        "w2ld"):
                del ev[k]
        for e_i in range(E):
            def emit_ctp(tc, ei=None):
                ei = e_i if ei is None else ei
                def ctp_fn(e, ei_=ei, t=tc):
                    ins = None
                    for tb in range(4):
                        ins = e.transpose(
                            pb[7][0:1, tb * 128:(tb + 1) * 128],
                            c_sb[:, t * 4 + tb, ei_:ei_ + 1], identf[:, :])
                    return ins
                wv = [("s_dve", ev["c", g]), ("s_c", v_c)]
                if tc == 1:
                    wv.append(("s_dve", ev["crw", g, ei, 0]))
                elif ei > 0:
                    wv.append(("s_dve", ev["crw", g, ei - 1, 1]))
                vct = emit("tensor", ctp_fn, wv, "s_pe")
                vcr = emit("vector",
                           lambda e: e.tensor_copy(
                               crow[:, :], pb[7][0:1, :]),
                           [("s_pe", vct)], "s_dve")
                ev["crw", g, ei, tc] = vcr

            def emit_cbc(tc, ei=None):
                ei = e_i if ei is None else ei
                wv = [("s_dve", ev["crw", g, ei, tc])]
                if tc == 1:
                    wv.append(("s_dve", ev["cbc", g, ei, 0]))
                elif ei > 0:
                    wv.append(("s_dve", ev["cbc", g, ei - 1, 1]))
                vcm = emit("tensor",
                           lambda e: e.matmul(
                               pb[6][:, :], onesf[:, :], crow[0:1, :],
                               start=True, stop=True),
                           wv, "s_pe")
                wv = [("s_pe", vcm)]
                if ei > 0:
                    wv.append(("s_dve", ev["yacc", ei - 1, NDT - 1, 1]))
                vcb = emit("vector",
                           lambda e, t=tc: e.tensor_copy(
                               cbc[:, t, :], pb[6][:, :]),
                           wv, "s_dve")
                ev["cbc", g, ei, tc] = vcb

            c_jobs = [(emit_ctp, 0), (emit_cbc, 0), (emit_ctp, 1),
                      (emit_cbc, 1)]

            for it in range(NIT):
                b = it % 2
                if ("w1ld", e_i, it) in ev:
                    vw1 = ev["w1ld", e_i, it]
                    vw3 = ev["w3ld", e_i, it]
                else:
                    vw1 = dma("gpsimd", "s_w1%d" % b, w1s[:, b, :, :],
                              w1_d[e_i, it, :, :, :], w1_free[b])
                    vw3 = dma("gpsimd", "s_w3%d" % b, w3s[:, b, :, :],
                              w3_d[e_i, it, :, :, :], w3_free[b])

                def gate_fn(e, bb_=b):
                    ins = None
                    for k in range(NK):
                        for tc in range(2):
                            ins = e.matmul(
                                pb[tc][:, :], w1s[:, bb_, k, :],
                                xnT[:, k, tc * TC:(tc + 1) * TC],
                                start=(k == 0), stop=(k == NK - 1))
                    return ins
                wv = [("s_w1%d" % b, vw1), ("s_dve", ev["xnTe", g * 8 + 7])]
                if ("silu", e_i, it - 1) in ev:
                    wv.append(("s_act", ev["silu", e_i, it - 1]))
                elif e_i > 0:
                    wv.append(("s_act", ev["silu", e_i - 1, NIT - 1]))
                elif g > 0:
                    wv.append(("s_dve", ev["oadd", g * 8 - 1]))
                vg = emit("tensor", gate_fn, wv, "s_pe")
                ev["gate", e_i, it] = vg
                w1_free[b] = [("s_pe", vg)]

                def up_fn(e, bb_=b):
                    ins = None
                    for k in range(NK):
                        for tc in range(2):
                            ins = e.matmul(
                                pb[2 + tc][:, :], w3s[:, bb_, k, :],
                                xnT[:, k, tc * TC:(tc + 1) * TC],
                                start=(k == 0), stop=(k == NK - 1))
                    return ins
                wv = [("s_w3%d" % b, vw3)]
                if ("h", e_i, it - 1) in ev:
                    wv.append(("s_dve", ev["h", e_i, it - 1]))
                elif e_i > 0:
                    wv.append(("s_dve", ev["h", e_i - 1, NIT - 1]))
                elif g > 0:
                    wv.append(("s_dve", ev["prod", 2 * g - 1, NDT - 1]))
                vu = emit("tensor", up_fn, wv, "s_pe")
                ev["up", e_i, it] = vu
                w3_free[b] = [("s_pe", vu)]

                def silu_fn(e):
                    e.activation(silu_s[:, 0, :], pb[0][:, :], AF.Silu)
                    return e.activation(silu_s[:, 1, :], pb[1][:, :], AF.Silu)
                wv = [("s_pe", vg)]
                if ("h", e_i, it - 1) in ev:
                    wv.append(("s_dve", ev["h", e_i, it - 1]))
                elif e_i > 0:
                    wv.append(("s_dve", ev["h", e_i - 1, NIT - 1]))
                    wv.append(("s_dve", ev["yacc", e_i - 1, NDT - 1, 1]))
                elif prev_yacc is not None:
                    wv.append(("s_dve", prev_yacc))
                vs = emit("scalar", silu_fn, wv, "s_act")
                ev["silu", e_i, it] = vs

                def h_fn(e, i=it):
                    e.tensor_tensor(h_s[:, i, 0:TC], silu_s[:, 0, :],
                                    pb[2][:, :], ALU.mult)
                    return e.tensor_tensor(h_s[:, i, TC:G], silu_s[:, 1, :],
                                           pb[3][:, :], ALU.mult)
                vh = emit("vector", h_fn,
                          [("s_act", vs), ("s_pe", vu)], "s_dve")
                ev["h", e_i, it] = vh
                if it < len(c_jobs):
                    fn_c, tc_c = c_jobs[it]
                    fn_c(tc_c)
                if it == 4 and g + 1 < ngrp:
                    emit_stats((g + 1) * 8 + e_i)
                if 12 <= it < 14:
                    dt_p = it - 12
                    ev["w2ld", e_i, dt_p] = dma(
                        "gpsimd", "s_w2%d" % (dt_p % 2),
                        w2s[:, dt_p % 2, :, :], w2_d[e_i, dt_p, :, :, :],
                        w2_free[dt_p % 2])

            for dt in range(NDT):
                b = dt % 2
                if dt in (2, 4) and e_i + 1 < E:
                    itp = dt // 2 - 1
                    ev["w1ld", e_i + 1, itp] = dma(
                        "gpsimd", "s_w1%d" % (itp % 2),
                        w1s[:, itp % 2, :, :], w1_d[e_i + 1, itp, :, :, :],
                        w1_free[itp % 2])
                    ev["w3ld", e_i + 1, itp] = dma(
                        "gpsimd", "s_w3%d" % (itp % 2),
                        w3s[:, itp % 2, :, :], w3_d[e_i + 1, itp, :, :, :],
                        w3_free[itp % 2])
                if ("w2ld", e_i, dt) in ev:
                    vw2 = ev["w2ld", e_i, dt]
                else:
                    vw2 = dma("gpsimd", "s_w2%d" % b, w2s[:, b, :, :],
                              w2_d[e_i, dt, :, :, :], w2_free[b])

                for tc in range(2):
                    pbank = pb[4 + (2 * dt + tc) % 2]
                    wv = [("s_w2%d" % b, vw2),
                          ("s_dve", ev["h", e_i, NIT - 1])]
                    if ("ym", e_i, dt - 1, tc) in ev:
                        wv.append(("s_dve", ev["ym", e_i, dt - 1, tc]))
                    elif e_i > 0:
                        wv.append(("s_dve", ev["ym", e_i - 1, NDT - 1, tc]))
                    elif g > 0:
                        wv.append(("s_dve", ev["scan", 2 * g - 1,
                                   NNT - 2 + tc]))
                        wv.append(("s_dve", ev["xnTe", g * 8 + 7]))
                    if dt == 0:
                        def ya_fn(e, bb_=b, t=tc, p=pbank):
                            ins = None
                            for kk in range(NIT - 2):
                                ins = e.matmul(
                                    p[:, :], w2s[:, bb_, kk, :],
                                    h_s[:, kk, t * TC:(t + 1) * TC],
                                    start=(kk == 0), stop=False)
                            return ins
                        wv = [w if w != ("s_dve", ev["h", e_i, NIT - 1])
                              else ("s_dve", ev["h", e_i, NIT - 3])
                              for w in wv]
                        emit("tensor", ya_fn, wv, "s_pe")
                        def yb_fn(e, bb_=b, t=tc, p=pbank):
                            ins = None
                            for kk in range(NIT - 2, NIT):
                                ins = e.matmul(
                                    p[:, :], w2s[:, bb_, kk, :],
                                    h_s[:, kk, t * TC:(t + 1) * TC],
                                    start=False, stop=(kk == NIT - 1))
                            return ins
                        vy = emit("tensor", yb_fn,
                                  [("s_dve", ev["h", e_i, NIT - 1])], "s_pe")
                    else:
                        def y_fn(e, bb_=b, t=tc, p=pbank):
                            ins = None
                            for kk in range(NIT):
                                ins = e.matmul(
                                    p[:, :], w2s[:, bb_, kk, :],
                                    h_s[:, kk, t * TC:(t + 1) * TC],
                                    start=(kk == 0), stop=(kk == NIT - 1))
                            return ins
                        vy = emit("tensor", y_fn, wv, "s_pe")
                    ev["y", e_i, dt, tc] = vy

                    wv = [("s_pe", vy), ("s_dve", ev["cbc", g, e_i, 1])]
                    if ("yacc", e_i, dt, tc - 1) in ev:
                        wv.append(("s_dve", ev["yacc", e_i, dt, tc - 1]))
                    elif ("yacc", e_i, dt - 1, 1) in ev:
                        wv.append(("s_dve", ev["yacc", e_i, dt - 1, 1]))
                    if e_i == 0:
                        vm = emit("vector",
                                  lambda e, d=dt, t=tc, p=pbank:
                                  e.tensor_tensor(
                                      y_acc[:, d, t * TC:(t + 1) * TC],
                                      p[:, :], cbc[:, t, :], ALU.mult),
                                  wv, "s_dve")
                        vye = vm
                    else:
                        vm = emit("vector",
                                  lambda e, t=tc, p=pbank: e.tensor_tensor(
                                      silu_s[:, t, :], p[:, :], cbc[:, t, :],
                                      ALU.mult), wv, "s_dve")
                        vye = emit("vector",
                                   lambda e, d=dt, t=tc: e.tensor_add(
                                       y_acc[:, d, t * TC:(t + 1) * TC],
                                       y_acc[:, d, t * TC:(t + 1) * TC],
                                       silu_s[:, t, :]),
                                   [("s_dve", vm)], "s_dve")
                    ev["ym", e_i, dt, tc] = vm
                    ev["yacc", e_i, dt, tc] = vye
                w2_free[b] = [("s_pe", ev["y", e_i, dt, 1])]

        def emit_bx_scan(tc):
            gc = 2 * g + tc
            sl = gc % 2
            for nt in range(NNT):
                pbB = pb[4 + nt % 2]
                def bx_fn(e, n=nt, t=tc, p=pbB):
                    for k in range(NK):
                        e.matmul(p[:, :], wbw[:, k, n, :],
                                 xnT[:, k, t * TC:(t + 1) * TC],
                                 start=(k == 0), stop=False)
                    return e.matmul(p[:, :], bbrow[0:1, n, :], onesb[0:1, :],
                                    start=False, stop=True)
                wv = [("s_cw", v_cw), ("s_gp", v_init)]
                if nt < 2 and tc == 0:
                    wv.append(("s_dve", ev["ym", E - 1, NDT - 1, nt]))
                if ("scan", gc, nt - 2) in ev:
                    wv.append(("s_dve", ev["scan", gc, nt - 2]))
                elif tc == 1:
                    wv.append(("s_dve", ev["scan", gc - 1, NNT - 2 + nt]))
                vb = emit("tensor", bx_fn, wv, "s_pe")
                ev["bx", gc, nt] = vb

                wv = [("s_pe", vb), ("s_gp", v_init)]
                if gc > 1:
                    wv.append(("s_pe", ev["cmm", gc - 2, NDT - 1]))
                if ("scan", gc - 1, nt) in ev:
                    wv.append(("s_dve", ev["scan", gc - 1, nt]))
                vsc = emit("vector",
                           lambda e, n=nt, p=pbB, s=sl: e.tensor_tensor_scan(
                               hs_s[:, s, n, :],
                               asig[:, n:n + 1].broadcast_to([128, TC]),
                               p[:, :], carry[:, n:n + 1],
                               ALU.mult, ALU.add),
                           wv, "s_dve")
                vcy = emit("vector",
                           lambda e, n=nt, s=sl: e.tensor_copy(
                               carry[:, n:n + 1], hs_s[:, s, n, TC - 1:TC]),
                           [("s_dve", vsc)], "s_dve")
                ev["scan", gc, nt] = vcy

        def emit_cs(tc):
            gc = 2 * g + tc
            sl = gc % 2
            for tb in range(3):
                ot = gc * 4 + tb
                obuf = ot % 3
                ev["xload", ot] = dma(
                    "sync", "s_x%d" % obuf, out_s[:, obuf, :],
                    x_d[ot * 128:(ot + 1) * 128, :], outs_free[obuf])
            for dt in range(NDT):
                pbS = pb[2 + dt % 2]
                def c_fn(e, d=dt, p=pbS, s=sl):
                    for kn in range(NNT):
                        e.matmul(p[:, :], wcw[:, kn, d, :], hs_s[:, s, kn, :],
                                 start=(kn == 0), stop=False)
                    return e.matmul(p[:, :], cbrow[0:1, d, :], onesb[0:1, :],
                                    start=False, stop=True)
                wv = [("s_cw", v_cw), ("s_dve", ev["scan", gc, NNT - 1])]
                if ("prod", gc, dt - 2) in ev:
                    wv.append(("s_dve", ev["prod", gc, dt - 2]))
                elif tc == 1:
                    wv.append(("s_dve", ev["prod", gc - 1, NDT - 2 + dt]))
                elif dt < 2:
                    wv.append(("s_dve", ev["h", E - 1, NIT - 1]))
                vc = emit("tensor", c_fn, wv, "s_pe")
                ev["cmm", gc, dt] = vc

                wv = [("s_dve", ev["yacc", E - 1, dt, tc])]
                if gc > 0:
                    wv.append(("s_pe", ev["ptp", gc - 1, 3]))
                vgs = emit("scalar",
                           lambda e, d=dt, t=tc: e.activation(
                               prodf[:, d, :], y_acc[:, d, t * TC:(t + 1) * TC],
                               AF.Silu),
                           wv, "s_act")
                vp = emit("vector",
                          lambda e, d=dt, p=pbS: e.tensor_tensor(
                              prodf[:, d, :], prodf[:, d, :],
                              p[:, :], ALU.mult),
                          [("s_act", vgs), ("s_pe", vc)], "s_dve")
                ev["prod", gc, dt] = vp

        def emit_epi(tc):
            gc = 2 * g + tc
            for tb in range(4):
                ot = gc * 4 + tb
                obuf = ot % 3
                if ("xload", ot) in ev:
                    vxr = ev["xload", ot]
                else:
                    vxr = dma("sync", "s_x%d" % obuf, out_s[:, obuf, :],
                              x_d[ot * 128:(ot + 1) * 128, :],
                              outs_free[obuf])
                def ptp_fn(e, t=tb, b=tb % 2):
                    ins = None
                    for dt in range(NDT):
                        ins = e.transpose(
                            pbO[b][:, dt * 128:(dt + 1) * 128],
                            prodf[:, dt, t * 128:(t + 1) * 128], identb[:, :])
                    return ins
                wv = [("s_dve", ev["prod", gc, NDT - 1])]
                if ("oadd", ot - 2) in ev:
                    wv.append(("s_dve", ev["oadd", ot - 2]))
                if tb == 0 and tc == 0:
                    wv.append(("s_act", ev["silu", E - 1, NIT - 1]))
                vtp = emit("tensor", ptp_fn, wv, "s_pe")
                ev["ptp", gc, tb] = vtp

                voa = emit("vector",
                           lambda e, b=obuf, k=tb % 2: e.tensor_add(
                               out_s[:, b, :], out_s[:, b, :], pbO[k][:, :]),
                           [("s_pe", vtp), ("s_x%d" % obuf, vxr)], "s_dve")
                ev["oadd", ot] = voa
                vst = dma("sync", "s_o%d" % obuf,
                          out_d[ot * 128:(ot + 1) * 128, :],
                          out_s[:, obuf, :], [("s_dve", voa)])
                outs_free[obuf] = [("s_o%d" % obuf, vst)]

        emit_bx_scan(0)
        emit_bx_scan(1)
        emit_cs(0)
        emit_epi(0)
        emit_cs(1)
        emit_epi(1)

    emit("sync", lambda e: e.nop(),
         [("s_o0", ctr["s_o0"]), ("s_o1", ctr["s_o1"]),
          ("s_o2", ctr["s_o2"])])

    block = ctx.enter_context(nc.Block())

    def replay(name):
        def f(eng):
            for fn, waits, inc, n in sched[name]:
                for s, v in waits:
                    eng.wait_ge(sems[s], v)
                ins = fn(eng)
                if inc is not None:
                    ins.then_inc(sems[inc], n)
        return f

    block.sync(replay("sync"))
    block.gpsimd(replay("gpsimd"))
    block.tensor(replay("tensor"))
    block.vector(replay("vector"))
    block.scalar(replay("scalar"))
    ctx.close()
    return nc


def prep_inputs(L0_attn_x, L0_attn_log_A, L0_attn_B_w, L0_attn_B_b,
                L0_attn_C_w, L0_attn_C_b, L0_ffn_Wr, L0_ffn_br,
                L0_ffn_W1, L0_ffn_W3, L0_ffn_W2):
    C = np.ascontiguousarray
    f = np.float32
    shared = {
        "bwT": C(np.asarray(L0_attn_B_w, f).T.reshape(NK, 128, NNT, 128)
                 .transpose(1, 0, 2, 3)),
        "cwT": C(np.asarray(L0_attn_C_w, f).T.reshape(NNT, 128, NDT, 128)
                 .transpose(1, 0, 2, 3)),
        "wr": C(np.asarray(L0_ffn_Wr, f).reshape(NK, 128, E).transpose(1, 0, 2)),
        "br": C(np.asarray(L0_ffn_br, f).reshape(1, E)),
        "w1": C(np.asarray(L0_ffn_W1, f).reshape(E, NK, 128, NIT, 128)
                .transpose(0, 3, 2, 1, 4)),
        "w3": C(np.asarray(L0_ffn_W3, f).reshape(E, NK, 128, NIT, 128)
                .transpose(0, 3, 2, 1, 4)),
        "w2": C(np.asarray(L0_ffn_W2, f).reshape(E, NIT, 128, NDT, 128)
                .transpose(0, 3, 2, 1, 4)),
        "la": C(np.asarray(L0_attn_log_A, f).reshape(NNT, 128).T),
        "bb": C(np.asarray(L0_attn_B_b, f).reshape(1, NNT, 128)),
        "cb": C(np.asarray(L0_attn_C_b, f).reshape(1, NDT, 128)),
    }
    xs = np.asarray(L0_attn_x, f)
    in_maps = [dict(shared, x=C(xs[b])) for b in range(xs.shape[0])]
    return in_maps


def kernel(**inputs):
    in_maps = prep_inputs(**inputs)
    nc = build_program()
    res = run_bass_kernel_spmd(nc, in_maps, core_ids=list(range(8)))
    out = np.stack([np.asarray(r["out"], dtype=np.float32)
                    for r in res.results], axis=0)
    return out


# version 19 (final)
# speedup vs baseline: 1.1731x; 1.0334x over previous
import numpy as np
import ml_dtypes
from contextlib import ExitStack

import concourse.bass as bass
import concourse.mybir as mybir
from concourse.bass_utils import run_bass_kernel_spmd

F32 = mybir.dt.float32
BF16 = mybir.dt.bfloat16
AF = mybir.ActivationFunctionType
ALU = mybir.AluOpType
AX = mybir.AxisListType
BF = ml_dtypes.bfloat16

D = 1024
II = 2048
E = 8
NGRP = 4
G = 1024
TC = 512
NK = D // 128
NIT = II // 128
NDT = D // 128
NNT = D // 128
EPS = 1e-5

ENGS = ("sync", "gpsimd", "tensor", "vector", "scalar")


def build_program(ngrp=NGRP):
    T_ = ngrp * G
    nc = bass.Bass("TRN2", target_bir_lowering=False)

    x_d = nc.dram_tensor("x", [T_, D], F32, kind="ExternalInput")
    bw_d = nc.dram_tensor("bwT", [128, NK, NNT, 128], F32, kind="ExternalInput")
    cw_d = nc.dram_tensor("cwT", [128, NNT, NDT, 128], F32, kind="ExternalInput")
    wr_d = nc.dram_tensor("wr", [128, NK, E], F32, kind="ExternalInput")
    br_d = nc.dram_tensor("br", [1, E], F32, kind="ExternalInput")
    w1_d = nc.dram_tensor("w1", [E, NIT, 128, NK, 128], F32, kind="ExternalInput")
    w3_d = nc.dram_tensor("w3", [E, NIT, 128, NK, 128], F32, kind="ExternalInput")
    w2_d = nc.dram_tensor("w2", [E, NDT, 128, NIT, 128], F32, kind="ExternalInput")
    la_d = nc.dram_tensor("la", [128, NNT], F32, kind="ExternalInput")
    bb_d = nc.dram_tensor("bb", [1, NNT, 128], F32, kind="ExternalInput")
    cb_d = nc.dram_tensor("cb", [1, NDT, 128], F32, kind="ExternalInput")
    out_d = nc.dram_tensor("out", [T_, D], F32, kind="ExternalOutput")
    eyeb_d = nc.inline_tensor(np.eye(128, dtype=np.float32).astype(BF), "eyeb")
    eyef_d = nc.inline_tensor(np.eye(128, dtype=np.float32), "eyef")

    ctx = ExitStack()
    sbuf = lambda n, s, d: ctx.enter_context(nc.sbuf_tensor(n, s, d))
    psum = lambda n, s, d: ctx.enter_context(nc.psum_tensor(n, s, d))

    x_p = sbuf("x_p", [128, 2, D], F32)
    xn_all = sbuf("xn_all", [128, 8, D], BF16)
    xnT = sbuf("xnT", [128, NK, G], BF16)
    h_s = sbuf("h_s", [128, NIT, G], BF16)
    y_acc = sbuf("y_acc", [128, NDT, G], F32)
    prodf = sbuf("prodf", [128, NDT, TC], BF16)
    wbw = sbuf("wbw", [128, NK, NNT, 128], BF16)
    wcw = sbuf("wcw", [128, NNT, NDT, 128], BF16)
    wwr = sbuf("wwr", [128, NK, E], BF16)
    wbr = sbuf("wbr", [1, E], BF16)
    w1s = sbuf("w1s", [128, 2, NK, 128], BF16)
    w3s = sbuf("w3s", [128, 2, NK, 128], BF16)
    w2s = sbuf("w2s", [128, 2, NIT, 128], BF16)
    hs_s = sbuf("hs_s", [128, 2, NNT, TC], BF16)
    silu_s = sbuf("silu_s", [128, 2, TC], F32)
    cbc = sbuf("cbc", [128, 2, TC], F32)
    crow = sbuf("crow", [1, TC], F32)
    out_s = sbuf("out_s", [128, 3, D], F32)
    identb = sbuf("identb", [128, 128], BF16)
    identf = sbuf("identf", [128, 128], F32)
    onesb = sbuf("onesb", [1, TC], BF16)
    onesf = sbuf("onesf", [1, 128], F32)
    la_sb = sbuf("la_sb", [128, NNT], F32)
    asig = sbuf("asig", [128, NNT], F32)
    carry = sbuf("carry", [128, NNT], F32)
    stats = sbuf("stats", [128, 6], F32)
    l_sb = sbuf("l_sb", [128, 8, E], F32)
    m1_s = sbuf("m1_s", [128, 8, 1], F32)
    m2_s = sbuf("m2_s", [128, 8, 1], F32)
    mk1 = sbuf("mk1", [128, 8, E], F32)
    mk2 = sbuf("mk2", [128, 8, E], F32)
    w1r = sbuf("w1r", [128, 8, 1], F32)
    c_sb = sbuf("c_sb", [128, 8, E], F32)
    bbrow = sbuf("bbrow", [1, NNT, 128], BF16)
    cbrow = sbuf("cbrow", [1, NDT, 128], BF16)

    pb = [psum(f"pb{i}", [128, 512], F32) for i in range(8)]
    pbT = [pb[4][:, :].bitcast(BF16), pb[5][:, :].bitcast(BF16)]
    pbO = [pb[0][:, :].bitcast(BF16), pb[1][:, :].bitcast(BF16)]

    sems = {}
    for s in ("s_c", "s_cw", "s_x0", "s_x1", "s_x2", "s_p0", "s_p1",
              "s_o0", "s_o1", "s_o2",
              "s_w10", "s_w11", "s_w30", "s_w31", "s_w20", "s_w21",
              "s_pe", "s_dve", "s_act", "s_gp"):
        sems[s] = ctx.enter_context(nc.semaphore(s))

    sched = {e: [] for e in ENGS}
    ctr = {s: 0 for s in sems}
    last_wait = {e: {} for e in ENGS}

    def emit(eng, fn, waits=(), inc=None, n=1):
        wl = {}
        for s, v in waits:
            if v is None or v <= 0:
                continue
            if v > wl.get(s, -1):
                wl[s] = v
        wl = {s: v for s, v in wl.items() if v > last_wait[eng].get(s, 0)}
        for s, v in wl.items():
            last_wait[eng][s] = v
        sched[eng].append((fn, sorted(wl.items()), inc, n))
        if inc is not None:
            ctr[inc] += n
            return ctr[inc]
        return None

    def dma(eng, sem, out_ap, in_ap, waits=()):
        return emit(eng, lambda e, o=out_ap, i=in_ap: e.dma_start(o, i),
                    waits, sem, 16)

    dma("sync", "s_c", la_sb[:, :], la_d[:, :])
    dma("sync", "s_c", identb[:, :], eyeb_d[:, :])
    v_c = dma("sync", "s_c", identf[:, :], eyef_d[:, :])

    dma("gpsimd", "s_cw", wbw[:, :, :, :], bw_d[:, :, :, :])
    dma("gpsimd", "s_cw", wcw[:, :, :, :], cw_d[:, :, :, :])
    dma("gpsimd", "s_cw", wwr[:, :, :], wr_d[:, :, :])
    dma("gpsimd", "s_cw", wbr[:, :], br_d[:, :])
    dma("gpsimd", "s_cw", bbrow[:, :, :], bb_d[:, :, :])
    v_cw = dma("gpsimd", "s_cw", cbrow[:, :, :], cb_d[:, :, :])

    emit("gpsimd", lambda e: e.memset(onesb[:, :], 1.0), (), "s_gp")
    emit("gpsimd", lambda e: e.memset(onesf[:, :], 1.0), (), "s_gp")
    v_init = emit("gpsimd", lambda e: e.memset(carry[:, :], 0.0), (), "s_gp")

    v_asig = emit("scalar",
                  lambda e: e.activation(asig[:, :], la_sb[:, :], AF.Sigmoid),
                  [("s_c", v_c)], "s_act")

    ev = {}
    xp_free = {0: [], 1: []}
    outs_free = {0: [], 1: [], 2: []}
    w1_free = {0: [], 1: []}
    w3_free = {0: [], 1: []}
    w2_free = {0: [], 1: []}

    def emit_stats_load(gt):
        buf = gt % 2
        ev["xpl", gt] = dma("sync", "s_p%d" % buf, x_p[:, buf, :],
                            x_d[gt * 128:(gt + 1) * 128, :], xp_free[buf])

    def emit_stats(gt):
        buf = gt % 2
        ti = gt % 8
        vx = ev["xpl", gt]
        v1 = emit("vector",
                  lambda e, b=buf: e.reduce_sum(
                      stats[:, b:b + 1], x_p[:, b, :], AX.X, negate=True),
                  [("s_p%d" % buf, vx)], "s_dve")
        v2 = emit("vector",
                  lambda e, b=buf: e.tensor_single_scalar(
                      stats[:, b:b + 1], stats[:, b:b + 1], 1.0 / D, ALU.mult),
                  [("s_dve", v1)], "s_dve")
        wv = [("s_dve", v2)]
        if ("tp", gt - 8) in ev:
            wv.append(("s_pe", ev["tp", gt - 8]))
        v3 = emit("scalar",
                  lambda e, b=buf, t=ti: e.activation(
                      xn_all[:, t, :], x_p[:, b, :], AF.Square,
                      bias=stats[:, b:b + 1], scale=1.0,
                      accum_out=stats[:, 2 + b:3 + b]),
                  wv, "s_act")
        v4a = emit("vector",
                   lambda e, b=buf: e.tensor_scalar(
                       stats[:, 4 + b:5 + b], stats[:, 2 + b:3 + b], 1.0 / D, EPS,
                       ALU.mult, ALU.add),
                   [("s_act", v3)], "s_dve")
        v4 = emit("scalar",
                  lambda e, b=buf: e.activation(
                      stats[:, 4 + b:5 + b], stats[:, 4 + b:5 + b], AF.Sqrt),
                  [("s_dve", v4a)], "s_act")
        v5 = emit("vector",
                  lambda e, b=buf: e.reciprocal(
                      stats[:, 4 + b:5 + b], stats[:, 4 + b:5 + b]),
                  [("s_act", v4)], "s_dve")
        v6 = emit("vector",
                  lambda e, b=buf, t=ti: e.tensor_scalar(
                      xn_all[:, t, :], x_p[:, b, :], stats[:, b:b + 1],
                      stats[:, 4 + b:5 + b], ALU.add, ALU.mult),
                  [("s_dve", v5), ("s_act", v4)], "s_dve")
        ev["xn", gt] = v6
        xp_free[buf] = [("s_dve", v6)]

    for g in range(ngrp):
        if g == 0:
            emit_stats_load(0)
            emit_stats_load(1)
            for ti in range(8):
                emit_stats(ti)
                if ti + 2 < 8:
                    emit_stats_load(ti + 2)
        for ti in range(8):
            gt = g * 8 + ti
            buf = ti % 2

            def tp_fn(e, t=ti, b=buf):
                ins = None
                for db in range(NK):
                    ins = e.transpose(pbT[b][:, db * 128:(db + 1) * 128],
                                      xn_all[:, t, db * 128:(db + 1) * 128],
                                      identb[:, :])
                return ins
            wv = [("s_dve", ev["xn", gt]), ("s_c", v_c)]
            if ("xnTe", gt - 2) in ev:
                wv.append(("s_dve", ev["xnTe", gt - 2]))
            if g > 0:
                wv.append(("s_dve", ev["scan", 2 * g - 1, NNT - 2 + buf]))
            v7 = emit("tensor", tp_fn, wv, "s_pe")
            ev["tp", gt] = v7
            v8 = emit("vector",
                      lambda e, b=buf, t=ti: e.tensor_copy(
                          xnT[:, :, t * 128:(t + 1) * 128],
                          pbT[b][:, :].rearrange("p (k m) -> p k m", k=NK)),
                      [("s_pe", v7)], "s_dve")
            ev["xnTe", gt] = v8

        for ti in range(8):
            gt = g * 8 + ti
            pl = pb[6 + (ti % 2)]
            def lg_fn(e, t=ti, p=pl):
                for k in range(NK):
                    e.matmul(p[:, 0:E], xnT[:, k, t * 128:(t + 1) * 128],
                             wwr[:, k, :], start=(k == 0), stop=False)
                return e.matmul(p[:, 0:E], onesb[0:1, 0:128], wbr[:, :],
                                start=False, stop=True)
            wv = [("s_dve", ev["xnTe", gt]), ("s_cw", v_cw),
                  ("s_gp", v_init)]
            if ("lcp", gt - 2) in ev:
                wv.append(("s_dve", ev["lcp", gt - 2]))
            if g > 0:
                wv.append(("s_dve", ev["cbc", g - 1, E - 1, 1]))
                wv.append(("s_dve", ev["crw", g - 1, E - 1, 1]))
            v9 = emit("tensor", lg_fn, wv, "s_pe")
            v10 = emit("vector",
                       lambda e, t=ti, p=pl: e.tensor_copy(
                           l_sb[:, t, :], p[:, 0:E]),
                       [("s_pe", v9)], "s_dve")
            ev["lcp", gt] = v10

        last_l = ev["lcp", g * 8 + 7]
        r = emit("vector", lambda e: e.reduce_max(
            m1_s[:, :, :], l_sb[:, :, :], AX.X), [("s_dve", last_l)], "s_dve")
        r = emit("vector", lambda e: e.tensor_tensor(
            mk1[:, :, :], l_sb[:, :, :], m1_s[:, :, :].broadcast_to([128, 8, E]),
            ALU.is_ge), [("s_dve", r)], "s_dve")
        r = emit("vector", lambda e: e.scalar_tensor_tensor(
            l_sb[:, :, :], mk1[:, :, :], -1e30, l_sb[:, :, :],
            ALU.mult, ALU.add), [("s_dve", r)], "s_dve")
        r = emit("vector", lambda e: e.reduce_max(
            m2_s[:, :, :], l_sb[:, :, :], AX.X), [("s_dve", r)], "s_dve")
        r = emit("vector", lambda e: e.tensor_tensor(
            mk2[:, :, :], l_sb[:, :, :], m2_s[:, :, :].broadcast_to([128, 8, E]),
            ALU.is_ge), [("s_dve", r)], "s_dve")
        r = emit("vector", lambda e: e.tensor_sub(
            m2_s[:, :, :], m1_s[:, :, :], m2_s[:, :, :]), [("s_dve", r)], "s_dve")
        r7 = emit("scalar", lambda e: e.activation(
            w1r[:, :, :], m2_s[:, :, :], AF.Sigmoid), [("s_dve", r)], "s_act")
        r = emit("vector", lambda e: e.tensor_scalar(
            m1_s[:, :, :], w1r[:, :, :], -1.0, 1.0, ALU.mult, ALU.add),
            [("s_act", r7)], "s_dve")
        r = emit("vector", lambda e: e.tensor_tensor(
            mk2[:, :, :], mk2[:, :, :], m1_s[:, :, :].broadcast_to([128, 8, E]),
            ALU.mult), [("s_dve", r)], "s_dve")
        r = emit("vector", lambda e: e.tensor_tensor(
            c_sb[:, :, :], mk1[:, :, :], w1r[:, :, :].broadcast_to([128, 8, E]),
            ALU.mult), [("s_dve", r)], "s_dve")
        r = emit("vector", lambda e: e.tensor_add(
            c_sb[:, :, :], c_sb[:, :, :], mk2[:, :, :]), [("s_dve", r)], "s_dve")
        ev["c", g] = r

        prev_yacc = ev.get(("yacc", E - 1, NDT - 1, 1))
        for k in list(ev):
            if k[0] in ("gate", "up", "silu", "h", "y", "ym", "yacc",
                        "w2ld"):
                del ev[k]
        for e_i in range(E):
            def emit_ctp(tc, ei=None):
                ei = e_i if ei is None else ei
                def ctp_fn(e, ei_=ei, t=tc):
                    ins = None
                    for tb in range(4):
                        ins = e.transpose(
                            pb[7][0:1, tb * 128:(tb + 1) * 128],
                            c_sb[:, t * 4 + tb, ei_:ei_ + 1], identf[:, :])
                    return ins
                wv = [("s_dve", ev["c", g]), ("s_c", v_c)]
                if tc == 1:
                    wv.append(("s_dve", ev["crw", g, ei, 0]))
                elif ei > 0:
                    wv.append(("s_dve", ev["crw", g, ei - 1, 1]))
                vct = emit("tensor", ctp_fn, wv, "s_pe")
                vcr = emit("vector",
                           lambda e: e.tensor_copy(
                               crow[:, :], pb[7][0:1, :]),
                           [("s_pe", vct)], "s_dve")
                ev["crw", g, ei, tc] = vcr

            def emit_cbc(tc, ei=None):
                ei = e_i if ei is None else ei
                wv = [("s_dve", ev["crw", g, ei, tc])]
                if tc == 1:
                    wv.append(("s_dve", ev["cbc", g, ei, 0]))
                elif ei > 0:
                    wv.append(("s_dve", ev["cbc", g, ei - 1, 1]))
                vcm = emit("tensor",
                           lambda e: e.matmul(
                               pb[6][:, :], onesf[:, :], crow[0:1, :],
                               start=True, stop=True),
                           wv, "s_pe")
                wv = [("s_pe", vcm)]
                if ei > 0:
                    wv.append(("s_dve", ev["yacc", ei - 1, NDT - 1, 1]))
                vcb = emit("vector",
                           lambda e, t=tc: e.tensor_copy(
                               cbc[:, t, :], pb[6][:, :]),
                           wv, "s_dve")
                ev["cbc", g, ei, tc] = vcb

            c_jobs = [(emit_ctp, 0), (emit_cbc, 0), (emit_ctp, 1),
                      (emit_cbc, 1)]

            for it in range(NIT):
                b = it % 2
                if ("w1ld", e_i, it) in ev:
                    vw1 = ev["w1ld", e_i, it]
                    vw3 = ev["w3ld", e_i, it]
                else:
                    vw1 = dma("gpsimd", "s_w1%d" % b, w1s[:, b, :, :],
                              w1_d[e_i, it, :, :, :], w1_free[b])
                    vw3 = dma("gpsimd", "s_w3%d" % b, w3s[:, b, :, :],
                              w3_d[e_i, it, :, :, :], w3_free[b])

                def gate_fn(e, bb_=b):
                    ins = None
                    for k in range(NK):
                        for tc in range(2):
                            ins = e.matmul(
                                pb[tc][:, :], w1s[:, bb_, k, :],
                                xnT[:, k, tc * TC:(tc + 1) * TC],
                                start=(k == 0), stop=(k == NK - 1))
                    return ins
                wv = [("s_w1%d" % b, vw1), ("s_dve", ev["xnTe", g * 8 + 7])]
                if ("silu", e_i, it - 1) in ev:
                    wv.append(("s_act", ev["silu", e_i, it - 1]))
                elif e_i > 0:
                    wv.append(("s_act", ev["silu", e_i - 1, NIT - 1]))
                elif g > 0:
                    wv.append(("s_dve", ev["oadd", g * 8 - 1]))
                vg = emit("tensor", gate_fn, wv, "s_pe")
                ev["gate", e_i, it] = vg
                w1_free[b] = [("s_pe", vg)]

                def up_fn(e, bb_=b):
                    ins = None
                    for k in range(NK):
                        for tc in range(2):
                            ins = e.matmul(
                                pb[2 + tc][:, :], w3s[:, bb_, k, :],
                                xnT[:, k, tc * TC:(tc + 1) * TC],
                                start=(k == 0), stop=(k == NK - 1))
                    return ins
                wv = [("s_w3%d" % b, vw3)]
                if ("h", e_i, it - 1) in ev:
                    wv.append(("s_dve", ev["h", e_i, it - 1]))
                elif e_i > 0:
                    wv.append(("s_dve", ev["h", e_i - 1, NIT - 1]))
                elif g > 0:
                    wv.append(("s_dve", ev["prod", 2 * g - 1, NDT - 1]))
                vu = emit("tensor", up_fn, wv, "s_pe")
                ev["up", e_i, it] = vu
                w3_free[b] = [("s_pe", vu)]

                def silu_fn(e):
                    e.activation(silu_s[:, 0, :], pb[0][:, :], AF.Silu)
                    return e.activation(silu_s[:, 1, :], pb[1][:, :], AF.Silu)
                wv = [("s_pe", vg)]
                if ("h", e_i, it - 1) in ev:
                    wv.append(("s_dve", ev["h", e_i, it - 1]))
                elif e_i > 0:
                    wv.append(("s_dve", ev["h", e_i - 1, NIT - 1]))
                    wv.append(("s_dve", ev["yacc", e_i - 1, NDT - 1, 1]))
                elif prev_yacc is not None:
                    wv.append(("s_dve", prev_yacc))
                vs = emit("scalar", silu_fn, wv, "s_act")
                ev["silu", e_i, it] = vs

                def h_fn(e, i=it):
                    e.tensor_tensor(h_s[:, i, 0:TC], silu_s[:, 0, :],
                                    pb[2][:, :], ALU.mult)
                    return e.tensor_tensor(h_s[:, i, TC:G], silu_s[:, 1, :],
                                           pb[3][:, :], ALU.mult)
                vh = emit("vector", h_fn,
                          [("s_act", vs), ("s_pe", vu)], "s_dve")
                ev["h", e_i, it] = vh
                if it < len(c_jobs):
                    fn_c, tc_c = c_jobs[it]
                    fn_c(tc_c)
                if it == 0 and g + 1 < ngrp:
                    emit_stats_load((g + 1) * 8 + e_i)
                if it == 6 and g + 1 < ngrp:
                    emit_stats((g + 1) * 8 + e_i)
                if 12 <= it < 14:
                    dt_p = it - 12
                    ev["w2ld", e_i, dt_p] = dma(
                        "gpsimd", "s_w2%d" % (dt_p % 2),
                        w2s[:, dt_p % 2, :, :], w2_d[e_i, dt_p, :, :, :],
                        w2_free[dt_p % 2])

            for dt in range(NDT):
                b = dt % 2
                if dt in (2, 4) and e_i + 1 < E:
                    itp = dt // 2 - 1
                    ev["w1ld", e_i + 1, itp] = dma(
                        "gpsimd", "s_w1%d" % (itp % 2),
                        w1s[:, itp % 2, :, :], w1_d[e_i + 1, itp, :, :, :],
                        w1_free[itp % 2])
                    ev["w3ld", e_i + 1, itp] = dma(
                        "gpsimd", "s_w3%d" % (itp % 2),
                        w3s[:, itp % 2, :, :], w3_d[e_i + 1, itp, :, :, :],
                        w3_free[itp % 2])
                if ("w2ld", e_i, dt) in ev:
                    vw2 = ev["w2ld", e_i, dt]
                else:
                    vw2 = dma("gpsimd", "s_w2%d" % b, w2s[:, b, :, :],
                              w2_d[e_i, dt, :, :, :], w2_free[b])

                for tc in range(2):
                    pbank = pb[4 + (2 * dt + tc) % 2]
                    wv = [("s_w2%d" % b, vw2),
                          ("s_dve", ev["h", e_i, NIT - 1])]
                    if ("ym", e_i, dt - 1, tc) in ev:
                        wv.append(("s_dve", ev["ym", e_i, dt - 1, tc]))
                    elif e_i > 0:
                        wv.append(("s_dve", ev["ym", e_i - 1, NDT - 1, tc]))
                    elif g > 0:
                        wv.append(("s_dve", ev["scan", 2 * g - 1,
                                   NNT - 2 + tc]))
                        wv.append(("s_dve", ev["xnTe", g * 8 + 7]))
                    if dt == 0:
                        def ya_fn(e, bb_=b, t=tc, p=pbank):
                            ins = None
                            for kk in range(NIT - 2):
                                ins = e.matmul(
                                    p[:, :], w2s[:, bb_, kk, :],
                                    h_s[:, kk, t * TC:(t + 1) * TC],
                                    start=(kk == 0), stop=False)
                            return ins
                        wv = [w if w != ("s_dve", ev["h", e_i, NIT - 1])
                              else ("s_dve", ev["h", e_i, NIT - 3])
                              for w in wv]
                        emit("tensor", ya_fn, wv, "s_pe")
                        def yb_fn(e, bb_=b, t=tc, p=pbank):
                            ins = None
                            for kk in range(NIT - 2, NIT):
                                ins = e.matmul(
                                    p[:, :], w2s[:, bb_, kk, :],
                                    h_s[:, kk, t * TC:(t + 1) * TC],
                                    start=False, stop=(kk == NIT - 1))
                            return ins
                        vy = emit("tensor", yb_fn,
                                  [("s_dve", ev["h", e_i, NIT - 1])], "s_pe")
                    else:
                        def y_fn(e, bb_=b, t=tc, p=pbank):
                            ins = None
                            for kk in range(NIT):
                                ins = e.matmul(
                                    p[:, :], w2s[:, bb_, kk, :],
                                    h_s[:, kk, t * TC:(t + 1) * TC],
                                    start=(kk == 0), stop=(kk == NIT - 1))
                            return ins
                        vy = emit("tensor", y_fn, wv, "s_pe")
                    ev["y", e_i, dt, tc] = vy

                    wv = [("s_pe", vy), ("s_dve", ev["cbc", g, e_i, 1])]
                    if ("yacc", e_i, dt, tc - 1) in ev:
                        wv.append(("s_dve", ev["yacc", e_i, dt, tc - 1]))
                    elif ("yacc", e_i, dt - 1, 1) in ev:
                        wv.append(("s_dve", ev["yacc", e_i, dt - 1, 1]))
                    if e_i == 0:
                        vm = emit("vector",
                                  lambda e, d=dt, t=tc, p=pbank:
                                  e.tensor_tensor(
                                      y_acc[:, d, t * TC:(t + 1) * TC],
                                      p[:, :], cbc[:, t, :], ALU.mult),
                                  wv, "s_dve")
                        vye = vm
                    else:
                        vm = emit("vector",
                                  lambda e, t=tc, p=pbank: e.tensor_tensor(
                                      silu_s[:, t, :], p[:, :], cbc[:, t, :],
                                      ALU.mult), wv, "s_dve")
                        vye = emit("vector",
                                   lambda e, d=dt, t=tc: e.tensor_add(
                                       y_acc[:, d, t * TC:(t + 1) * TC],
                                       y_acc[:, d, t * TC:(t + 1) * TC],
                                       silu_s[:, t, :]),
                                   [("s_dve", vm)], "s_dve")
                    ev["ym", e_i, dt, tc] = vm
                    ev["yacc", e_i, dt, tc] = vye
                w2_free[b] = [("s_pe", ev["y", e_i, dt, 1])]

        def emit_bx_scan(tc):
            gc = 2 * g + tc
            sl = gc % 2
            for nt in range(NNT):
                pbB = pb[4 + nt % 2]
                def bx_fn(e, n=nt, t=tc, p=pbB):
                    for k in range(NK):
                        e.matmul(p[:, :], wbw[:, k, n, :],
                                 xnT[:, k, t * TC:(t + 1) * TC],
                                 start=(k == 0), stop=False)
                    return e.matmul(p[:, :], bbrow[0:1, n, :], onesb[0:1, :],
                                    start=False, stop=True)
                wv = [("s_cw", v_cw), ("s_gp", v_init)]
                if nt < 2 and tc == 0:
                    wv.append(("s_dve", ev["ym", E - 1, NDT - 1, nt]))
                if ("scan", gc, nt - 2) in ev:
                    wv.append(("s_dve", ev["scan", gc, nt - 2]))
                elif tc == 1:
                    wv.append(("s_dve", ev["scan", gc - 1, NNT - 2 + nt]))
                vb = emit("tensor", bx_fn, wv, "s_pe")
                ev["bx", gc, nt] = vb

                wv = [("s_pe", vb), ("s_gp", v_init)]
                if gc > 1:
                    wv.append(("s_pe", ev["cmm", gc - 2, NDT - 1]))
                if ("scan", gc - 1, nt) in ev:
                    wv.append(("s_dve", ev["scan", gc - 1, nt]))
                vsc = emit("vector",
                           lambda e, n=nt, p=pbB, s=sl: e.tensor_tensor_scan(
                               hs_s[:, s, n, :],
                               asig[:, n:n + 1].broadcast_to([128, TC]),
                               p[:, :], carry[:, n:n + 1],
                               ALU.mult, ALU.add),
                           wv, "s_dve")
                vcy = emit("vector",
                           lambda e, n=nt, s=sl: e.tensor_copy(
                               carry[:, n:n + 1], hs_s[:, s, n, TC - 1:TC]),
                           [("s_dve", vsc)], "s_dve")
                ev["scan", gc, nt] = vcy

        def emit_cs(tc):
            gc = 2 * g + tc
            sl = gc % 2
            for tb in range(3):
                ot = gc * 4 + tb
                obuf = ot % 3
                ev["xload", ot] = dma(
                    "sync", "s_x%d" % obuf, out_s[:, obuf, :],
                    x_d[ot * 128:(ot + 1) * 128, :], outs_free[obuf])
            for dt in range(NDT):
                pbS = pb[2 + dt % 2]
                def c_fn(e, d=dt, p=pbS, s=sl):
                    for kn in range(NNT):
                        e.matmul(p[:, :], wcw[:, kn, d, :], hs_s[:, s, kn, :],
                                 start=(kn == 0), stop=False)
                    return e.matmul(p[:, :], cbrow[0:1, d, :], onesb[0:1, :],
                                    start=False, stop=True)
                wv = [("s_cw", v_cw), ("s_dve", ev["scan", gc, NNT - 1])]
                if ("prod", gc, dt - 2) in ev:
                    wv.append(("s_dve", ev["prod", gc, dt - 2]))
                elif tc == 1:
                    wv.append(("s_dve", ev["prod", gc - 1, NDT - 2 + dt]))
                elif dt < 2:
                    wv.append(("s_dve", ev["h", E - 1, NIT - 1]))
                vc = emit("tensor", c_fn, wv, "s_pe")
                ev["cmm", gc, dt] = vc

                wv = [("s_dve", ev["yacc", E - 1, dt, tc])]
                if gc > 0:
                    wv.append(("s_pe", ev["ptp", gc - 1, 3]))
                vgs = emit("scalar",
                           lambda e, d=dt, t=tc: e.activation(
                               prodf[:, d, :], y_acc[:, d, t * TC:(t + 1) * TC],
                               AF.Silu),
                           wv, "s_act")
                vp = emit("vector",
                          lambda e, d=dt, p=pbS: e.tensor_tensor(
                              prodf[:, d, :], prodf[:, d, :],
                              p[:, :], ALU.mult),
                          [("s_act", vgs), ("s_pe", vc)], "s_dve")
                ev["prod", gc, dt] = vp

        def emit_epi(tc):
            gc = 2 * g + tc
            for tb in range(4):
                ot = gc * 4 + tb
                obuf = ot % 3
                if ("xload", ot) in ev:
                    vxr = ev["xload", ot]
                else:
                    vxr = dma("sync", "s_x%d" % obuf, out_s[:, obuf, :],
                              x_d[ot * 128:(ot + 1) * 128, :],
                              outs_free[obuf])
                def ptp_fn(e, t=tb, b=tb % 2):
                    ins = None
                    for dt in range(NDT):
                        ins = e.transpose(
                            pbO[b][:, dt * 128:(dt + 1) * 128],
                            prodf[:, dt, t * 128:(t + 1) * 128], identb[:, :])
                    return ins
                wv = [("s_dve", ev["prod", gc, NDT - 1])]
                if ("oadd", ot - 2) in ev:
                    wv.append(("s_dve", ev["oadd", ot - 2]))
                if tb == 0 and tc == 0:
                    wv.append(("s_act", ev["silu", E - 1, NIT - 1]))
                vtp = emit("tensor", ptp_fn, wv, "s_pe")
                ev["ptp", gc, tb] = vtp

                voa = emit("vector",
                           lambda e, b=obuf, k=tb % 2: e.tensor_add(
                               out_s[:, b, :], out_s[:, b, :], pbO[k][:, :]),
                           [("s_pe", vtp), ("s_x%d" % obuf, vxr)], "s_dve")
                ev["oadd", ot] = voa
                vst = dma("sync", "s_o%d" % obuf,
                          out_d[ot * 128:(ot + 1) * 128, :],
                          out_s[:, obuf, :], [("s_dve", voa)])
                outs_free[obuf] = [("s_o%d" % obuf, vst)]

        emit_bx_scan(0)
        emit_bx_scan(1)
        emit_cs(0)
        emit_epi(0)
        emit_cs(1)
        emit_epi(1)

    emit("sync", lambda e: e.nop(),
         [("s_o0", ctr["s_o0"]), ("s_o1", ctr["s_o1"]),
          ("s_o2", ctr["s_o2"])])

    block = ctx.enter_context(nc.Block())

    def replay(name):
        def f(eng):
            for fn, waits, inc, n in sched[name]:
                for s, v in waits:
                    eng.wait_ge(sems[s], v)
                ins = fn(eng)
                if inc is not None:
                    ins.then_inc(sems[inc], n)
        return f

    block.sync(replay("sync"))
    block.gpsimd(replay("gpsimd"))
    block.tensor(replay("tensor"))
    block.vector(replay("vector"))
    block.scalar(replay("scalar"))
    ctx.close()
    return nc


def prep_inputs(L0_attn_x, L0_attn_log_A, L0_attn_B_w, L0_attn_B_b,
                L0_attn_C_w, L0_attn_C_b, L0_ffn_Wr, L0_ffn_br,
                L0_ffn_W1, L0_ffn_W3, L0_ffn_W2):
    C = np.ascontiguousarray
    f = np.float32
    shared = {
        "bwT": C(np.asarray(L0_attn_B_w, f).T.reshape(NK, 128, NNT, 128)
                 .transpose(1, 0, 2, 3)),
        "cwT": C(np.asarray(L0_attn_C_w, f).T.reshape(NNT, 128, NDT, 128)
                 .transpose(1, 0, 2, 3)),
        "wr": C(np.asarray(L0_ffn_Wr, f).reshape(NK, 128, E).transpose(1, 0, 2)),
        "br": C(np.asarray(L0_ffn_br, f).reshape(1, E)),
        "w1": C(np.asarray(L0_ffn_W1, f).reshape(E, NK, 128, NIT, 128)
                .transpose(0, 3, 2, 1, 4)),
        "w3": C(np.asarray(L0_ffn_W3, f).reshape(E, NK, 128, NIT, 128)
                .transpose(0, 3, 2, 1, 4)),
        "w2": C(np.asarray(L0_ffn_W2, f).reshape(E, NIT, 128, NDT, 128)
                .transpose(0, 3, 2, 1, 4)),
        "la": C(np.asarray(L0_attn_log_A, f).reshape(NNT, 128).T),
        "bb": C(np.asarray(L0_attn_B_b, f).reshape(1, NNT, 128)),
        "cb": C(np.asarray(L0_attn_C_b, f).reshape(1, NDT, 128)),
    }
    xs = np.asarray(L0_attn_x, f)
    in_maps = [dict(shared, x=C(xs[b])) for b in range(xs.shape[0])]
    return in_maps


def kernel(**inputs):
    in_maps = prep_inputs(**inputs)
    nc = build_program()
    res = run_bass_kernel_spmd(nc, in_maps, core_ids=list(range(8)))
    out = np.stack([np.asarray(r["out"], dtype=np.float32)
                    for r in res.results], axis=0)
    return out
